# version 01
# baseline (speedup 1.0000x reference)
import math
import numpy as np
import concourse.bass as bass
import concourse.mybir as mybir
from concourse.bass_utils import run_bass_kernel_spmd

F32 = mybir.dt.float32
BF16 = mybir.dt.bfloat16
AF = mybir.ActivationFunctionType
ALU = mybir.AluOpType
AX = mybir.AxisListType

NCORE = 8
NEG = -30000.0
CSH = 30.0



class Prog:
    ENGS = ["sync", "tensor", "vector", "scalar", "gpsimd"]

    def __init__(self, nc, sem):
        self.nc = nc
        self.S = sem
        self.items = {e: [] for e in self.ENGS}
        self.total = 0

    def _push(self, eng, fn, inc, wait):
        self.items[eng].append((wait, fn, inc))
        self.total += inc

    def op(self, eng, fn, inc=1):
        self._push(eng, fn, inc, self.total)

    def group(self, eng, fns):
        w = self.total
        n = len(fns)
        for i, fn in enumerate(fns):
            self._push(eng, fn, 1 if i == n - 1 else 0, w if i == 0 else None)

    def dma(self, out, in_, eng="sync", **kw):
        def fn(E, o=out, i=in_, k=kw):
            return E.dma_start(out=o, in_=i, **k)
        self._push(eng, fn, 16, self.total)

    def finalize(self, blk):
        tot = self.total
        S = self.S
        for eng in self.ENGS:
            its = self.items[eng]
            if not its and eng != "sync":
                continue

            def body(E, its=its, final=(eng == "sync")):
                for (wait, fn, inc) in its:
                    if wait is not None:
                        E.wait_ge(S, wait)
                    ins = fn(E)
                    if inc:
                        ins.then_inc(S, inc)
                if final:
                    E.wait_ge(S, tot)
            getattr(blk, eng)(body)


def OP(p, eng, meth, **kw):
    def fn(E, m=meth, k=kw):
        return getattr(E, m)(**k)
    p.op(eng, fn)


def MMS(p, lst):
    fns = []
    for k in lst:
        def fn(E, k=k):
            return E.matmul(**k)
        fns.append(fn)
    p.group("tensor", fns)


def TRANS(p, pairs, ident, ptb):
    for d, s in pairs:
        OP(p, "tensor", "transpose", out=ptb, in_=s, identity=ident)
        OP(p, "vector", "tensor_copy", out=d, in_=ptb)


def LN(p, out, x, tmp, STD_, D, eps=1e-5):
    st = STD_.st
    OP(p, "vector", "tensor_reduce", out=st[:, 0:1], in_=x, axis=AX.X, op=ALU.add)
    OP(p, "scalar", "mul", out=st[:, 0:1], in_=st[:, 0:1], mul=1.0 / D)
    OP(p, "scalar", "square", out=tmp, in_=x)
    OP(p, "vector", "tensor_reduce", out=st[:, 1:2], in_=tmp, axis=AX.X, op=ALU.add)
    OP(p, "scalar", "mul", out=st[:, 1:2], in_=st[:, 1:2], mul=1.0 / D)
    OP(p, "vector", "tensor_mul", out=st[:, 2:3], in0=st[:, 0:1], in1=st[:, 0:1])
    OP(p, "vector", "tensor_sub", out=st[:, 1:2], in0=st[:, 1:2], in1=st[:, 2:3])
    OP(p, "scalar", "activation", out=st[:, 2:3], in_=st[:, 1:2], func=AF.Ln, bias=CVAL(p, STD_, eps))
    OP(p, "scalar", "activation", out=st[:, 3:4], in_=st[:, 2:3], func=AF.Exp, scale=-0.5)
    OP(p, "vector", "tensor_scalar", out=tmp, in0=x, scalar1=st[:, 0:1], scalar2=None, op0=ALU.subtract)
    OP(p, "vector", "tensor_scalar", out=out, in0=tmp, scalar1=st[:, 3:4], scalar2=None, op0=ALU.mult)


def RMS(p, out, x, tmp, STD_, D, eps, gain_bc=None):
    st = STD_.st
    OP(p, "scalar", "square", out=tmp, in_=x)
    OP(p, "vector", "tensor_reduce", out=st[:, 0:1], in_=tmp, axis=AX.X, op=ALU.add)
    OP(p, "scalar", "mul", out=st[:, 0:1], in_=st[:, 0:1], mul=1.0 / D)
    OP(p, "scalar", "activation", out=st[:, 1:2], in_=st[:, 0:1], func=AF.Ln, bias=CVAL(p, STD_, eps))
    OP(p, "scalar", "activation", out=st[:, 2:3], in_=st[:, 1:2], func=AF.Exp, scale=-0.5)
    OP(p, "vector", "tensor_scalar", out=out, in0=x, scalar1=st[:, 2:3], scalar2=None, op0=ALU.mult)
    if gain_bc is not None:
        OP(p, "vector", "tensor_mul", out=out, in0=out, in1=gain_bc)


def BCAST(p, std, dst, row_hbm):
    F = dst.shape[-1]
    p.dma(out=std.wstage[0:1, 0:F], in_=row_hbm[0:1, 0:F])
    o = 0
    while o < F:
        n = min(512, F - o)
        MMS(p, [dict(out=std.ps[0][:, 0:n], lhsT=std.on1, rhs=std.wstage[0:1, o:o + n],
                     start=True, stop=True)])
        OP(p, "scalar", "copy", out=dst[:, o:o + n], in_=std.ps[0][:, 0:n])
        o += n


def WLOAD(p, Wsb, w_hbm, chunks, O, stage, row0=0):
    r = row0
    for c, cn in enumerate(chunks):
        o = 0
        while o < O:
            n = min(stage.shape[-1], O - o)
            p.dma(out=stage[0:cn, 0:n], in_=w_hbm[r:r + cn, o:o + n])
            OP(p, "vector", "tensor_copy", out=Wsb[0:cn, c * O + o:c * O + o + n], in_=stage[0:cn, 0:n])
            o += n
        r += cn


def TLIN(p, xT, Wsb, IC, O, ps_list, evac, ocw=512):
    nb = (O + ocw - 1) // ocw
    for ob in range(nb):
        o0 = ob * ocw
        on = min(ocw, O - o0)
        ps = ps_list[ob % len(ps_list)]
        MMS(p, [dict(out=ps[:, 0:on], lhsT=xT[c], rhs=Wsb[:, c * O + o0:c * O + o0 + on],
                     start=(c == 0), stop=(c == IC - 1)) for c in range(IC)])
        evac(ob, o0, on, ps)


def ceil_div(a, b):
    return (a + b - 1) // b


def chunks_of(I):
    ch = [128] * (I // 128)
    if I % 128:
        ch.append(I % 128)
    return ch


def hT(w):
    return np.ascontiguousarray(np.asarray(w, np.float32).T)


def hrow(v):
    return np.ascontiguousarray(np.asarray(v, np.float32).reshape(1, -1))


_BUILD_CACHE = {}


def _cached(key, fn):
    if key not in _BUILD_CACHE:
        _BUILD_CACHE[key] = fn()
    return _BUILD_CACHE[key]


def _consts():
    return {
        "ident": np.eye(128, dtype=np.float32),
        "ones1": np.ones((1, 128), dtype=np.float32),
        "onescol": np.ones((128, 1), dtype=np.float32),
    }


def _std_tensors(nc):
    d = {}
    d["ident"] = nc.dram_tensor("ident", [128, 128], F32, kind="ExternalInput").ap()
    d["ones1"] = nc.dram_tensor("ones1", [1, 128], F32, kind="ExternalInput").ap()
    d["onescol"] = nc.dram_tensor("onescol", [128, 1], F32, kind="ExternalInput").ap()
    return d


class Std:

    def __init__(self, nc, es):
        sb = nc.sbuf_tensor
        self.idf = es.enter_context(sb("c_idf", [128, 128], F32))[:]
        self.idb = es.enter_context(sb("c_idb", [128, 128], BF16))[:]
        self.on1 = es.enter_context(sb("c_on1", [1, 128], F32))[:]
        self.onb = es.enter_context(sb("c_onb", [128, 1], BF16))[:]
        self.onstage = es.enter_context(sb("c_onstage", [128, 1], F32))[:]
        self.st = es.enter_context(sb("c_st", [128, 16], F32))[:]
        self.r1 = es.enter_context(sb("c_r1", [1, 512], F32))[:]
        self.rb = es.enter_context(sb("c_rb", [128, 512], F32))[:]
        self.tmpE = es.enter_context(sb("c_tmpE", [128, 512], BF16))[:]
        self.tmpF = es.enter_context(sb("c_tmpF", [128, 1024], F32))[:]
        self.tmpF2 = es.enter_context(sb("c_tmpF2", [128, 1024], F32))[:]
        self.xt = es.enter_context(sb("c_xt", [128, 1024], F32))[:]
        self.xb = es.enter_context(sb("c_xb", [128, 1024], BF16))[:]
        self.xT = es.enter_context(sb("c_xT", [128, 8 * 128], BF16))[:]
        self.wstage = es.enter_context(sb("c_wstage", [128, 1024], F32))[:]
        self.cv = es.enter_context(sb("c_cv", [128, 12], F32))[:]
        self.cvmap = {}
        self.tmpE2 = es.enter_context(sb("c_tmpE2", [128, 512], BF16))[:]
        self.ptb = es.enter_context(nc.psum_tensor("c_ptb", [128, 128], BF16))[:]
        self.ptf = es.enter_context(nc.psum_tensor("c_ptf", [128, 128], F32))[:]
        self.ps = [es.enter_context(nc.psum_tensor("c_ps%d" % i, [128, 512], F32))[:] for i in range(4)]

    def init(self, p, cin):
        p.dma(out=self.idf, in_=cin["ident"])
        p.dma(out=self.on1, in_=cin["ones1"])
        p.dma(out=self.onstage, in_=cin["onescol"])
        OP(p, "vector", "tensor_copy", out=self.idb, in_=self.idf)
        OP(p, "vector", "tensor_copy", out=self.onb, in_=self.onstage)


def CVAL(p, std, v):
    v = float(v)
    if v not in std.cvmap:
        col = len(std.cvmap)
        std.cvmap[v] = col
        OP(p, "vector", "memset", ap=std.cv[:, col:col + 1], constant=v)
    c = std.cvmap[v]
    return std.cv[:, c:c + 1]


def XTC(xT, IC=8):
    return [xT[:, c * 128:(c + 1) * 128] for c in range(IC)]


def load_and_prep_tile(p, std, src_row0, src, I=1024, pre=None, prebc=None, pre2bc=None,
                       eps=1e-5, want_f32=None):
    p.dma(out=std.xt[:, 0:I], in_=src[src_row0:src_row0 + 128, 0:I])
    cur = std.xt[:, 0:I]
    if pre == "ln":
        LN(p, std.tmpF[:, 0:I], cur, std.tmpF2[:, 0:I], std, I, eps)
        cur = std.tmpF[:, 0:I]
    elif pre == "rms":
        RMS(p, std.tmpF[:, 0:I], cur, std.tmpF2[:, 0:I], std, I, eps)
        cur = std.tmpF[:, 0:I]
    if prebc is not None:
        OP(p, "vector", "tensor_mul", out=cur, in0=cur, in1=prebc)
    if pre2bc is not None:
        OP(p, "vector", "tensor_add", out=cur, in0=cur, in1=pre2bc)
    OP(p, "vector", "tensor_copy", out=std.xb[:, 0:I], in_=cur)
    IC = ceil_div(I, 128)
    TRANS(p, [(std.xT[:, c * 128:(c + 1) * 128], std.xb[:, c * 128:min(I, (c + 1) * 128)])
              for c in range(IC)], std.idb, std.ptb)
    return cur




def ATTN(p, std, cfg):
    nq, nk, nv = cfg["nq"], cfg["nk"], cfg["nv"]
    kts, qsb, vsb, csb = cfg["ktsb"], cfg["qsb"], cfg["vsb"], cfg["csb"]
    for hi, hd in enumerate(cfg["heads"]):
        dh, dv = hd["dh"], hd["dv"]
        nkc = ceil_div(dh, 128)
        dvc = ceil_div(dv, 128)
        for c in range(nkc):
            cn = min(128, dh - c * 128)
            p.dma(out=kts[0:cn, c * nk:c * nk + nk],
                  in_=cfg["KT"][hd["k_off"] + c * 128:hd["k_off"] + c * 128 + cn, 0:nk])
            p.dma(out=qsb[0:cn, c * nq:c * nq + nq],
                  in_=cfg["QT"][hd["q_off"] + c * 128:hd["q_off"] + c * 128 + cn, 0:nq])
        ntile = ceil_div(nv, 128)
        for t in range(ntile):
            tn = min(128, nv - t * 128)
            p.dma(out=vsb[0:tn, t * dv:t * dv + dv],
                  in_=cfg["V"][t * 128:t * 128 + tn, hd["v_off"]:hd["v_off"] + dv])
        negm = None
        if cfg.get("maxpass"):
            mx = std.st[:, 8:9]
            first = True
            for qi2 in range(ceil_div(nq, 128)):
                q0 = qi2 * 128
                qn = min(128, nq - q0)
                for (k0, kn, maskq) in cfg["stat_tiles"](hi, q0, qn):
                    MMS(p, [dict(out=std.ps[0][0:qn, 0:kn],
                                 lhsT=qsb[0:min(128, dh - c * 128), c * nq + q0:c * nq + q0 + qn],
                                 rhs=kts[0:min(128, dh - c * 128), c * nk + k0:c * nk + k0 + kn],
                                 start=(c == 0), stop=(c == nkc - 1)) for c in range(nkc)])
                    if maskq is not None:
                        OP(p, "vector", "tensor_tensor", out=std.rb[0:qn, 0:kn],
                           in0=std.ps[0][0:qn, 0:kn], in1=maskq[0:qn, 0:kn], op=ALU.add)
                        src = std.rb[0:qn, 0:kn]
                    else:
                        src = std.ps[0][0:qn, 0:kn]
                    OP(p, "vector", "tensor_reduce", out=std.st[0:qn, 9:10], in_=src,
                       axis=AX.X, op=ALU.max)
                    if first:
                        OP(p, "vector", "tensor_copy", out=mx[0:qn], in_=std.st[0:qn, 9:10])
                        first = False
                    else:
                        OP(p, "vector", "tensor_tensor", out=mx[0:qn], in0=mx[0:qn],
                           in1=std.st[0:qn, 9:10], op=ALU.max)
            OP(p, "tensor", "transpose", out=std.ptf[0:1, 0:128], in_=mx, identity=std.idf)
            OP(p, "vector", "tensor_reduce", out=std.r1[0:1, 0:1], in_=std.ptf[0:1, 0:128],
               axis=AX.X, op=ALU.max)
            MMS(p, [dict(out=std.ptf[0:128, 0:1], lhsT=std.on1, rhs=std.r1[0:1, 0:1],
                         start=True, stop=True)])
            OP(p, "scalar", "mul", out=std.st[:, 10:11], in_=std.ptf[0:128, 0:1], mul=-1.0)
            negm = std.st[:, 10:11]
        for (qi, (q0, qn)) in enumerate(cfg["qblocks"]):
            po = [std.ps[0], std.ps[3]][:dvc]
            pz = std.ps[1]
            pss = std.ps[2]
            if cfg.get("maxpass") == "qb":
                mx = std.st[:, 8:9]
                first = True
                for qt in range(ceil_div(qn, 128)):
                    sq0 = q0 + qt * 128
                    sqn = min(128, q0 + qn - sq0)
                    for (k0, kn, maskq) in cfg["stat_tiles"](hi, sq0, sqn):
                        MMS(p, [dict(out=std.ps[0][0:sqn, 0:kn],
                                     lhsT=qsb[0:min(128, dh - c * 128), c * nq + sq0:c * nq + sq0 + sqn],
                                     rhs=kts[0:min(128, dh - c * 128), c * nk + k0:c * nk + k0 + kn],
                                     start=(c == 0), stop=(c == nkc - 1)) for c in range(nkc)])
                        if maskq is not None:
                            OP(p, "vector", "tensor_tensor", out=std.rb[0:sqn, 0:kn],
                               in0=std.ps[0][0:sqn, 0:kn], in1=maskq[0:sqn, 0:kn], op=ALU.add)
                            src_ = std.rb[0:sqn, 0:kn]
                        else:
                            src_ = std.ps[0][0:sqn, 0:kn]
                        OP(p, "vector", "tensor_reduce", out=std.st[0:sqn, 9:10], in_=src_,
                           axis=AX.X, op=ALU.max)
                        if first:
                            OP(p, "vector", "tensor_copy", out=mx[0:sqn], in_=std.st[0:sqn, 9:10])
                            first = False
                        else:
                            OP(p, "vector", "tensor_tensor", out=mx[0:sqn], in0=mx[0:sqn],
                               in1=std.st[0:sqn, 9:10], op=ALU.max)
                OP(p, "tensor", "transpose", out=std.ptf[0:1, 0:128], in_=mx, identity=std.idf)
                OP(p, "vector", "tensor_reduce", out=std.r1[0:1, 0:1], in_=std.ptf[0:1, 0:128],
                   axis=AX.X, op=ALU.max)
                MMS(p, [dict(out=std.ptf[0:128, 0:1], lhsT=std.on1, rhs=std.r1[0:1, 0:1],
                             start=True, stop=True)])
                OP(p, "scalar", "mul", out=std.st[:, 10:11], in_=std.ptf[0:128, 0:1], mul=-1.0)
                negm = std.st[:, 10:11]
            tls = cfg["tiles"](hi, qi, q0, qn)
            nt = len(tls)
            acc = 0
            tot = sum(kn for (_, kn, _, _) in tls)
            for ti, (k0, kn, maskT, vpieces) in enumerate(tls):
                MMS(p, [dict(out=pss[0:kn, 0:qn],
                             lhsT=kts[0:min(128, dh - c * 128), c * nk + k0:c * nk + k0 + kn],
                             rhs=qsb[0:min(128, dh - c * 128), c * nq + q0:c * nq + q0 + qn],
                             start=(c == 0), stop=(c == nkc - 1)) for c in range(nkc)])
                if maskT is not None:
                    OP(p, "vector", "tensor_tensor", out=std.rb[0:kn, 0:qn],
                       in0=pss[0:kn, 0:qn], in1=maskT[0:kn, 0:qn], op=ALU.add)
                    esrc = std.rb[0:kn, 0:qn]
                else:
                    esrc = pss[0:kn, 0:qn]
                bias = negm[0:kn, 0:1] if negm is not None else CVAL(p, std, -CSH)[0:kn, 0:1]
                OP(p, "scalar", "activation", out=std.tmpE[0:kn, 0:qn], in_=esrc,
                   func=AF.Exp, bias=bias)
                pieces = []
                er = 0
                for (vr0, take0) in vpieces:
                    vr = vr0
                    rem = take0
                    while rem > 0:
                        t = vr // 128
                        off = vr % 128
                        take = min(128 - off, rem)
                        pieces.append((t, off, take, er))
                        er += take
                        vr += take
                        rem -= take
                for (t, off, take, er) in pieces:
                    if er != off:
                        OP(p, "vector", "tensor_copy", out=std.tmpE2[off:off + take, 0:qn],
                           in_=std.tmpE[er:er + take, 0:qn])
                mm = []
                for (t, off, take, er) in pieces:
                    rhsE = std.tmpE[er:er + take, 0:qn] if er == off else std.tmpE2[off:off + take, 0:qn]
                    st_ = (acc + er == 0)
                    sp_ = (acc + er + take == tot)
                    for dc in range(dvc):
                        dvn = min(128, dv - dc * 128)
                        mm.append(dict(out=po[dc][0:dvn, 0:qn],
                                       lhsT=vsb[off:off + take, t * dv + dc * 128:t * dv + dc * 128 + dvn],
                                       rhs=rhsE, start=st_, stop=sp_))
                    mm.append(dict(out=pz[0:1, 0:qn],
                                   lhsT=std.onb[off:off + take, 0:1],
                                   rhs=rhsE, start=st_, stop=sp_))
                acc += kn
                MMS(p, mm)
            OP(p, "vector", "reciprocal", out=std.r1[0:1, 0:qn], in_=pz[0:1, 0:qn])
            MMS(p, [dict(out=pss[0:128, 0:qn], lhsT=std.on1, rhs=std.r1[0:1, 0:qn],
                         start=True, stop=True)])
            OP(p, "scalar", "copy", out=std.rb[0:128, 0:qn], in_=pss[0:128, 0:qn])
            for dc in range(dvc):
                dvn = min(128, dv - dc * 128)
                OP(p, "vector", "tensor_tensor", out=csb[0:dvn, 0:qn],
                   in0=po[dc][0:dvn, 0:qn], in1=std.rb[0:dvn, 0:qn], op=ALU.mult)
                p.dma(out=cfg["CT"][hd["o_off"] + dc * 128:hd["o_off"] + dc * 128 + dvn, q0:q0 + qn],
                      in_=csb[0:dvn, 0:qn])


def PROJ_FM(p, std, xnT, N, Wsb, O, dst_hbm, bias_col=None, scale=1.0, dst_col0=0,
            dst_row0=0, IC=8, I=1024):
    for oc in range(ceil_div(O, 128)):
        on = min(128, O - oc * 128)
        for s0 in range(0, N, 512):
            sn = min(512, N - s0)
            ps = std.ps[0]
            MMS(p, [dict(out=ps[0:on, 0:sn], lhsT=Wsb[:, c * O + oc * 128:c * O + oc * 128 + on],
                         rhs=xnT[:, c * N + s0:c * N + s0 + sn],
                         start=(c == 0), stop=(c == IC - 1)) for c in range(IC)])
            if bias_col is not None and scale != 1.0:
                OP(p, "vector", "tensor_scalar", out=std.tmpE[0:on, 0:sn], in0=ps[0:on, 0:sn],
                   scalar1=bias_col[0:on, oc:oc + 1], scalar2=scale, op0=ALU.add, op1=ALU.mult)
            elif bias_col is not None:
                OP(p, "vector", "tensor_scalar", out=std.tmpE[0:on, 0:sn], in0=ps[0:on, 0:sn],
                   scalar1=bias_col[0:on, oc:oc + 1], scalar2=None, op0=ALU.add)
            elif scale != 1.0:
                OP(p, "vector", "tensor_scalar", out=std.tmpE[0:on, 0:sn], in0=ps[0:on, 0:sn],
                   scalar1=scale, scalar2=None, op0=ALU.mult)
            else:
                OP(p, "scalar", "copy", out=std.tmpE[0:on, 0:sn], in_=ps[0:on, 0:sn])
            p.dma(out=dst_hbm[dst_row0 + oc * 128:dst_row0 + oc * 128 + on,
                              dst_col0 + s0:dst_col0 + s0 + sn],
                  in_=std.tmpE[0:on, 0:sn])


def TLIN2(p, xT, cns, Wsb, O, ps_list, evac, ocw=512, extra=None):
    IC = len(cns)
    nb = (O + ocw - 1) // ocw
    for ob in range(nb):
        o0 = ob * ocw
        on = min(ocw, O - o0)
        ps = ps_list[ob % len(ps_list)]
        mm = []
        tot = IC + (len(extra) if extra else 0)
        idx = 0
        for c in range(IC):
            cn = cns[c]
            mm.append(dict(out=ps[0:128, 0:on], lhsT=xT[c][0:cn, :],
                           rhs=Wsb[0:cn, c * O + o0:c * O + o0 + on],
                           start=(idx == 0), stop=(idx == tot - 1)))
            idx += 1
        if extra:
            for (lh, rh, Of) in extra:
                mm.append(dict(out=ps[0:128, 0:on], lhsT=lh, rhs=rh[:, o0:o0 + on],
                               start=(idx == 0), stop=(idx == tot - 1)))
                idx += 1
        MMS(p, mm)
        evac(ob, o0, on, ps)


class B:
    def __init__(self):
        from contextlib import ExitStack
        self.nc = bass.Bass("TRN2", target_bir_lowering=False)
        self.cin = _std_tensors(self.nc)
        self.es = ExitStack()
        self.es.__enter__()
        self.S = self.es.enter_context(self.nc.semaphore("S"))
        self.p = Prog(self.nc, self.S)
        self.std = None

    def inp(self, name, shape, dt=F32):
        return self.nc.dram_tensor(name, list(shape), dt, kind="ExternalInput").ap()

    def outp(self, name, shape, dt=F32):
        return self.nc.dram_tensor(name, list(shape), dt, kind="ExternalOutput").ap()

    def internal(self, name, shape, dt=BF16):
        return self.nc.dram_tensor(name, list(shape), dt, kind="Internal").ap()

    def sb(self, name, shape, dt=BF16):
        return self.es.enter_context(self.nc.sbuf_tensor(name, list(shape), dt))[:]

    def begin(self):
        self.std = Std(self.nc, self.es)
        self.std.init(self.p, self.cin)

    def done(self):
        blk = self.es.enter_context(self.nc.Block())
        self.p.finalize(blk)
        self.es.__exit__(None, None, None)
        return self.nc


def ln_prep(b, src, t, nxT, N, double_norm=False, gain_bc=None):
    p, std = b.p, b.std
    p.dma(out=std.xt, in_=src[t * 128:(t + 1) * 128, :])
    LN(p, std.tmpF, std.xt, std.tmpF2, std, 1024)
    cur = std.tmpF
    if double_norm:
        RMS(p, std.xt, std.tmpF, std.tmpF2, std, 1024, 1e-6, gain_bc)
        cur = std.xt
    elif gain_bc is not None:
        OP(p, "vector", "tensor_mul", out=std.tmpF, in0=std.tmpF, in1=gain_bc)
    OP(p, "vector", "tensor_copy", out=std.xb, in_=cur)
    TRANS(p, [(nxT[:, c * N + t * 128:c * N + t * 128 + 128],
               std.xb[:, c * 128:(c + 1) * 128]) for c in range(8)], std.idb, std.ptb)
    return cur



def build_l0_attn(gi0, gi1):
    b = B()
    N = 4096
    x = b.inp("x", [N, 1024])
    wq = b.inp("wq", [1024, 1024]); wk = b.inp("wk", [1024, 1024])
    wv = b.inp("wv", [1024, 1024]); wo = b.inp("wo", [1024, 1024])
    bqc = b.inp("bqc", [128, 8]); bkc = b.inp("bkc", [128, 8])
    bvr = b.inp("bvr", [1, 1024]); bor = b.inp("bor", [1, 1024])
    mkl = b.inp("mkl", [5, 128, 512])
    so = b.outp("so", [N, 1024])
    QT = b.internal("QT", [1024, N]); KT = b.internal("KT", [1024, N + 128])
    V = b.internal("V", [N + 128, 1024])
    CA = b.internal("CA", [1024, N]); CB = b.internal("CB", [1024, N]); CC = b.internal("CC", [1024, N])
    b.begin()
    p, std = b.p, b.std
    nxT = b.sb("nxT", [128, 8 * N], BF16)
    wsb = b.sb("wsb", [128, 8 * 1024], BF16)
    bcol = b.sb("bcol", [128, 8], F32)
    bvbc = b.sb("bvbc", [128, 1024], F32)
    bobc = b.sb("bobc", [128, 1024], F32)
    mks = b.sb("mks", [128, 5 * 512], F32)
    ktsb = b.sb("ktsb", [128, N + 128], BF16)
    qsb = b.sb("qsb", [128, N], BF16)
    vsb = b.sb("vsb", [128, 33 * 64], BF16)
    csb = b.sb("csb", [128, 512], BF16)
    BCAST(p, std, bvbc, bvr)
    BCAST(p, std, bobc, bor)
    for j in range(5):
        p.dma(out=mks[:, j * 512:(j + 1) * 512], in_=mkl[j])
    for t in range(32):
        ln_prep(b, x, t, nxT, N)
    WLOAD(p, wsb, wq, [128] * 8, 1024, std.wstage)
    p.dma(out=bcol[:], in_=bqc)
    PROJ_FM(p, std, nxT, N, wsb, 1024, QT, bias_col=bcol, scale=0.125)
    WLOAD(p, wsb, wk, [128] * 8, 1024, std.wstage)
    p.dma(out=bcol[:], in_=bkc)
    PROJ_FM(p, std, nxT, N, wsb, 1024, KT, bias_col=bcol, dst_col0=64)
    p.dma(out=KT[:, 0:64], in_=KT[:, N:N + 64])
    p.dma(out=KT[:, N + 64:N + 128], in_=KT[:, 64:128])
    WLOAD(p, wsb, wv, [128] * 8, 1024, std.wstage)
    for t in range(32):
        xT = [nxT[:, c * N + t * 128:c * N + t * 128 + 128] for c in range(8)]

        def ev(ob, o0, on, ps, t=t):
            OP(p, "vector", "tensor_tensor", out=std.tmpE[:, 0:on], in0=ps[:, 0:on],
               in1=bvbc[:, o0:o0 + on], op=ALU.add)
            p.dma(out=V[64 + t * 128:64 + (t + 1) * 128, o0:o0 + on], in_=std.tmpE[:, 0:on])
        TLIN(p, xT, wsb, 8, 1024, std.ps, ev)
    p.dma(out=V[0:64, :], in_=V[N:N + 64, :])
    p.dma(out=V[N + 64:N + 128, :], in_=V[64:128, :])
    heads = [dict(q_off=h * 64, k_off=h * 64, v_off=h * 64, o_off=h * 64, dh=64, dv=64)
             for h in range(16)]

    def tiles_local(hi, qi, q0, qn):
        qb = q0 // 512
        return [(64 * (8 * qb + 2 * j), 128, mks[:, j * 512:(j + 1) * 512],
                 [(64 * (8 * qb + 2 * j), 128)]) for j in range(5)]
    cfg = dict(QT=QT, KT=KT, V=V, CT=CA, heads=heads, nq=N, nk=N + 128, nv=N + 128,
               qblocks=[(i * 512, 512) for i in range(8)], tiles=tiles_local,
               ktsb=ktsb, qsb=qsb, vsb=vsb, csb=csb)
    ATTN(p, std, cfg)

    def tiles_col(hi, qi, q0, qn):
        return [(64 * gi0 + 64, 64, None, [(64 * gi0 + 64, 64)]),
                (64 * gi1 + 64, 64, None, [(64 * gi1 + 64, 64)])]
    cfg2 = dict(cfg); cfg2["CT"] = CB; cfg2["tiles"] = tiles_col
    ATTN(p, std, cfg2)

    def tiles_rows(hi, qi, q0, qn):
        return [(k * 128, 128, None, [(k * 128, 128)]) for k in range(32)]
    cfg3 = dict(cfg); cfg3["CT"] = CC; cfg3["tiles"] = tiles_rows
    cfg3["qblocks"] = [(64 * gi0, 64), (64 * gi1, 64)]
    ATTN(p, std, cfg3)
    e1 = std.tmpE
    e2 = b.sb("e2", [128, 512], BF16)
    for c in range(8):
        for s0 in range(0, N, 512):
            p.dma(out=e1[:, 0:512], in_=CA[c * 128:(c + 1) * 128, s0:s0 + 512])
            p.dma(out=e2[:, 0:512], in_=CB[c * 128:(c + 1) * 128, s0:s0 + 512])
            OP(p, "vector", "tensor_add", out=e1[:, 0:512], in0=e1[:, 0:512], in1=e2[:, 0:512])
            for gi in (gi0, gi1):
                gc = 64 * gi
                if s0 <= gc < s0 + 512:
                    p.dma(out=e2[:, 0:64], in_=CC[c * 128:(c + 1) * 128, gc:gc + 64])
                    OP(p, "vector", "tensor_copy", out=e1[:, gc - s0:gc - s0 + 64], in_=e2[:, 0:64])
            p.dma(out=CA[c * 128:(c + 1) * 128, s0:s0 + 512], in_=e1[:, 0:512])
    WLOAD(p, wsb, wo, [128] * 8, 1024, std.wstage)
    for t in range(32):
        for c in range(8):
            p.dma(out=std.xT[:, c * 128:(c + 1) * 128],
                  in_=CA[c * 128:(c + 1) * 128, t * 128:(t + 1) * 128])
        xT = XTC(std.xT)

        def ev(ob, o0, on, ps, t=t):
            OP(p, "vector", "tensor_tensor", out=std.tmpF[:, o0:o0 + on], in0=ps[:, 0:on],
               in1=bobc[:, o0:o0 + on], op=ALU.add)
        TLIN(p, xT, wsb, 8, 1024, std.ps, ev)
        p.dma(out=so[t * 128:(t + 1) * 128, :], in_=std.tmpF)
    return b.done()



def build_l0f_a():
    b = B()
    N = 4096
    x = b.inp("x", [N, 1024])
    wq = b.inp("wq", [1024, 1024]); wkc = b.inp("wkc", [1024, 256])
    wvc = b.inp("wvc", [1024, 256]); wo = b.inp("wo", [1024, 1024])
    g1 = b.inp("g1", [1, 1024])
    xa = b.outp("xa", [N, 1024])
    QT = b.internal("QT", [1024, N]); KT = b.internal("KT", [256, N])
    V = b.internal("V", [N, 256]); CT = b.internal("CT", [1024, N])
    b.begin()
    p, std = b.p, b.std
    nxT = b.sb("nxT", [128, 8 * N], BF16)
    wsb = b.sb("wsb", [128, 8 * 1024], BF16)
    g1bc = b.sb("g1bc", [128, 1024], F32)
    ktsb = b.sb("ktsb", [128, 2 * N], BF16)
    qsb = b.sb("qsb", [128, 2 * N], BF16)
    vsb = b.sb("vsb", [128, 32 * 256], BF16)
    csb = b.sb("csb", [128, 512], BF16)
    BCAST(p, std, g1bc, g1)
    for t in range(32):
        ln_prep(b, x, t, nxT, N, double_norm=True, gain_bc=g1bc)
    WLOAD(p, wsb, wkc, [128] * 8, 256, std.wstage)
    PROJ_FM(p, std, nxT, N, wsb, 256, KT)
    WLOAD(p, wsb, wvc, [128] * 8, 256, std.wstage)
    for t in range(32):
        xT = [nxT[:, c * N + t * 128:c * N + t * 128 + 128] for c in range(8)]

        def ev(ob, o0, on, ps, t=t):
            OP(p, "scalar", "activation", out=std.tmpE[:, 0:on], in_=ps[:, 0:on], func=AF.Copy)
            p.dma(out=V[t * 128:(t + 1) * 128, o0:o0 + on], in_=std.tmpE[:, 0:on])
        TLIN(p, xT, wsb, 8, 256, std.ps, ev)
    WLOAD(p, wsb, wq, [128] * 8, 1024, std.wstage)
    PROJ_FM(p, std, nxT, N, wsb, 1024, QT, scale=0.0625)
    heads = [dict(q_off=h * 256, k_off=0, v_off=0, o_off=h * 256, dh=256, dv=256)
             for h in range(4)]
    cfg = dict(QT=QT, KT=KT, V=V, CT=CT, heads=heads, nq=N, nk=N, nv=N,
               qblocks=[(i * 512, 512) for i in range(8)],
               tiles=lambda hi, qi, q0, qn: [(k * 128, 128, None, [(k * 128, 128)]) for k in range(32)],
               ktsb=ktsb, qsb=qsb, vsb=vsb, csb=csb)
    ATTN(p, std, cfg)
    WLOAD(p, wsb, wo, [128] * 8, 1024, std.wstage)
    for t in range(32):
        for c in range(8):
            p.dma(out=std.xT[:, c * 128:(c + 1) * 128],
                  in_=CT[c * 128:(c + 1) * 128, t * 128:(t + 1) * 128])
        p.dma(out=std.xt, in_=x[t * 128:(t + 1) * 128, :])
        LN(p, std.tmpF2, std.xt, std.tmpF, std, 1024)

        def ev(ob, o0, on, ps, t=t):
            OP(p, "vector", "tensor_tensor", out=std.tmpF[:, o0:o0 + on], in0=ps[:, 0:on],
               in1=std.tmpF2[:, o0:o0 + on], op=ALU.add)
        TLIN(p, XTC(std.xT), wsb, 8, 1024, std.ps, ev)
        p.dma(out=xa[t * 128:(t + 1) * 128, :], in_=std.tmpF)
    return b.done()



def build_l0f_b(part):
    b = B()
    N = 4096
    xa = b.inp("xa", [N, 1024])
    g2 = b.inp("g2", [1, 1024])
    wr = b.inp("wr", [1024, 8]); ebr = b.inp("ebr", [1, 8])
    ew1 = b.inp("ew1", [4, 1024, 1024]); ew3 = b.inp("ew3", [4, 1024, 1024])
    ew2 = b.inp("ew2", [4, 1024, 1024])
    if part == 0:
        sw1 = b.inp("sw1", [1024, 1024]); sw2 = b.inp("sw2", [1024, 1024])
        sw3 = b.inp("sw3", [1024, 1024])
        z = b.outp("z", [N, 1024])
    else:
        z = b.inp("z", [N, 1024])
        x0 = b.inp("x0", [N, 1024]); s0 = b.inp("s0", [N, 1024])
        zo = b.outp("zo", [N, 1024])
        xo = b.outp("xo", [N, 1024])
    b.begin()
    p, std = b.p, b.std
    nxT = b.sb("nxT", [128, 8 * N], BF16)
    w1sb = b.sb("w1sb", [128, 8 * 1024], BF16)
    w3sb = b.sb("w3sb", [128, 8 * 1024], BF16)
    w2sb = b.sb("w2sb", [128, 8 * 1024], BF16)
    g2bc = b.sb("g2bc", [128, 1024], F32)
    ebbc = b.sb("ebbc", [128, 8], F32)
    cw = b.sb("cw", [128, 32 * 8], F32)
    r8a = b.sb("r8a", [128, 8], F32); r8b = b.sb("r8b", [128, 8], F32)
    r8c = b.sb("r8c", [128, 8], F32)
    hb = b.sb("hb", [128, 1024], BF16)
    zt = b.sb("zt", [128, 1024], F32)
    BCAST(p, std, g2bc, g2)
    BCAST(p, std, ebbc, ebr)
    wrsb = b.sb("wrsb", [128, 8 * 8], BF16)
    WLOAD(p, wrsb, wr, [128] * 8, 8, std.wstage)
    for t in range(32):
        ln_prep_rms_only(b, xa, t, nxT, N, g2bc)
        xT = [nxT[:, c * N + t * 128:c * N + t * 128 + 128] for c in range(8)]
        MMS(p, [dict(out=std.ps[0][:, 0:8], lhsT=xT[c], rhs=wrsb[:, c * 8:(c + 1) * 8],
                     start=(c == 0), stop=(c == 7)) for c in range(8)])
        OP(p, "vector", "tensor_tensor", out=r8a[:], in0=std.ps[0][:, 0:8], in1=ebbc[:], op=ALU.add)
        OP(p, "vector", "tensor_reduce", out=std.st[:, 0:1], in_=r8a[:], axis=AX.X, op=ALU.max)
        OP(p, "scalar", "mul", out=std.st[:, 1:2], in_=std.st[:, 0:1], mul=-1.0)
        OP(p, "scalar", "activation", out=r8a[:], in_=r8a[:], func=AF.Exp, bias=std.st[:, 1:2])
        OP(p, "vector", "tensor_reduce", out=std.st[:, 2:3], in_=r8a[:], axis=AX.X, op=ALU.add)
        OP(p, "vector", "reciprocal", out=std.st[:, 3:4], in_=std.st[:, 2:3])
        OP(p, "vector", "tensor_scalar", out=r8a[:], in0=r8a[:], scalar1=std.st[:, 3:4],
           scalar2=None, op0=ALU.mult)
        OP(p, "vector", "tensor_reduce", out=std.st[:, 4:5], in_=r8a[:], axis=AX.X, op=ALU.max)
        OP(p, "vector", "tensor_scalar", out=r8b[:], in0=r8a[:], scalar1=std.st[:, 4:5],
           scalar2=None, op0=ALU.is_ge)
        OP(p, "vector", "scalar_tensor_tensor", out=r8c[:], in0=r8b[:], scalar=-1e9,
           in1=r8a[:], op0=ALU.mult, op1=ALU.add)
        OP(p, "vector", "tensor_reduce", out=std.st[:, 5:6], in_=r8c[:], axis=AX.X, op=ALU.max)
        OP(p, "vector", "tensor_scalar", out=r8b[:], in0=r8a[:], scalar1=std.st[:, 5:6],
           scalar2=None, op0=ALU.is_ge)
        OP(p, "scalar", "activation", out=r8c[:], in_=r8a[:], func=AF.Exp)
        OP(p, "vector", "tensor_mul", out=r8c[:], in0=r8c[:], in1=r8b[:])
        OP(p, "vector", "tensor_reduce", out=std.st[:, 6:7], in_=r8c[:], axis=AX.X, op=ALU.add)
        OP(p, "vector", "reciprocal", out=std.st[:, 7:8], in_=std.st[:, 6:7])
        OP(p, "vector", "tensor_scalar", out=cw[:, t * 8:(t + 1) * 8], in0=r8c[:],
           scalar1=std.st[:, 7:8], scalar2=None, op0=ALU.mult)
    if part == 0:
        WLOAD(p, w1sb, sw1, [128] * 8, 1024, std.wstage)
        WLOAD(p, w3sb, sw3, [128] * 8, 1024, std.wstage)
        WLOAD(p, w2sb, sw2, [128] * 8, 1024, std.wstage)
        for t in range(32):
            xT = [nxT[:, c * N + t * 128:c * N + t * 128 + 128] for c in range(8)]

            def ev1(ob, o0, on, ps):
                OP(p, "scalar", "activation", out=std.tmpF[:, o0:o0 + on], in_=ps[:, 0:on], func=AF.Silu)
            TLIN(p, xT, w1sb, 8, 1024, std.ps, ev1)

            def ev3(ob, o0, on, ps):
                OP(p, "scalar", "copy", out=std.tmpF2[:, o0:o0 + on], in_=ps[:, 0:on])
            TLIN(p, xT, w3sb, 8, 1024, std.ps, ev3)
            OP(p, "vector", "tensor_mul", out=hb[:], in0=std.tmpF, in1=std.tmpF2)
            TRANS(p, [(std.xT[:, c * 128:(c + 1) * 128], hb[:, c * 128:(c + 1) * 128])
                      for c in range(8)], std.idb, std.ptb)

            def ev2(ob, o0, on, ps):
                OP(p, "scalar", "copy", out=std.tmpF[:, o0:o0 + on], in_=ps[:, 0:on])
            TLIN(p, XTC(std.xT), w2sb, 8, 1024, std.ps, ev2)
            p.dma(out=z[t * 128:(t + 1) * 128, :], in_=std.tmpF)
    for e in range(4):
        ge = part * 4 + e
        WLOAD(p, w1sb, ew1[e], [128] * 8, 1024, std.wstage)
        WLOAD(p, w3sb, ew3[e], [128] * 8, 1024, std.wstage)
        WLOAD(p, w2sb, ew2[e], [128] * 8, 1024, std.wstage)
        for t in range(32):
            xT = [nxT[:, c * N + t * 128:c * N + t * 128 + 128] for c in range(8)]

            def ev1(ob, o0, on, ps):
                OP(p, "scalar", "activation", out=std.tmpF[:, o0:o0 + on], in_=ps[:, 0:on], func=AF.Silu)
            TLIN(p, xT, w1sb, 8, 1024, std.ps, ev1)

            def ev3(ob, o0, on, ps):
                OP(p, "scalar", "copy", out=std.tmpF2[:, o0:o0 + on], in_=ps[:, 0:on])
            TLIN(p, xT, w3sb, 8, 1024, std.ps, ev3)
            OP(p, "vector", "tensor_mul", out=hb[:], in0=std.tmpF, in1=std.tmpF2)
            TRANS(p, [(std.xT[:, c * 128:(c + 1) * 128], hb[:, c * 128:(c + 1) * 128])
                      for c in range(8)], std.idb, std.ptb)

            def ev2(ob, o0, on, ps, t=t, ge=ge):
                OP(p, "vector", "tensor_scalar", out=std.tmpF[:, o0:o0 + on], in0=ps[:, 0:on],
                   scalar1=cw[:, t * 8 + ge:t * 8 + ge + 1], scalar2=None, op0=ALU.mult)
            TLIN(p, XTC(std.xT), w2sb, 8, 1024, std.ps, ev2)
            zdst = z if part == 0 else zo
            zsrc = z
            if part == 0 or e > 0:
                zsrc = zdst if (part == 1 and e > 0) else z
            p.dma(out=zt[:], in_=zsrc[t * 128:(t + 1) * 128, :])
            OP(p, "vector", "tensor_add", out=std.tmpF[:], in0=std.tmpF[:], in1=zt[:])
            p.dma(out=zdst[t * 128:(t + 1) * 128, :], in_=std.tmpF)
    if part == 1:
        for t in range(32):
            r = slice(t * 128, (t + 1) * 128)
            p.dma(out=zt[:], in_=zo[r, :])
            p.dma(out=std.xt, in_=xa[r, :])
            OP(p, "vector", "tensor_add", out=zt[:], in0=zt[:], in1=std.xt)
            OP(p, "scalar", "activation", out=zt[:], in_=zt[:], func=AF.Silu)
            p.dma(out=std.tmpF, in_=s0[r, :])
            OP(p, "vector", "tensor_mul", out=zt[:], in0=zt[:], in1=std.tmpF)
            p.dma(out=std.tmpF2, in_=x0[r, :])
            OP(p, "vector", "tensor_add", out=zt[:], in0=zt[:], in1=std.tmpF2)
            p.dma(out=xo[r, :], in_=zt[:])
    return b.done()


def ln_prep_rms_only(b, src, t, nxT, N, gain_bc):
    p, std = b.p, b.std
    p.dma(out=std.xt, in_=src[t * 128:(t + 1) * 128, :])
    RMS(p, std.tmpF, std.xt, std.tmpF2, std, 1024, 1e-6, gain_bc)
    OP(p, "vector", "tensor_copy", out=std.xb, in_=std.tmpF)
    TRANS(p, [(nxT[:, c * N + t * 128:c * N + t * 128 + 128],
               std.xb[:, c * 128:(c + 1) * 128]) for c in range(8)], std.idb, std.ptb)



def build_l1_attn(h0, withproj):
    b = B()
    N = 4096
    x = b.inp("x", [N, 1024])
    wq = b.inp("wq", [1024, 512])
    wk = b.inp("wk", [1024, 128]); wv = b.inp("wv", [1024, 128])
    qwr = b.inp("qwr", [1, 512]); kwr = b.inp("kwr", [1, 128])
    rc = b.inp("rc", [N, 512]); rs = b.inp("rs", [N, 512])
    ctxo = b.outp("ctxo", [512, N], BF16)
    if withproj:
        wo = b.inp("wo", [1024, 1024])
        ctxa = b.inp("ctxa", [512, N], BF16)
        so = b.outp("so", [N, 1024])
    QT = b.internal("QT", [512, N]); KT = b.internal("KT", [128, N])
    V = b.internal("V", [N, 128])
    b.begin()
    p, std = b.p, b.std
    nxT = b.sb("nxT", [128, 8 * N], BF16)
    wsb = b.sb("wsb", [128, 8 * 1024], BF16)
    qwbc = b.sb("qwbc", [128, 512], F32)
    kwbc = b.sb("kwbc", [128, 128], F32)
    rcb = b.sb("rcb", [128, 512], F32); rsb = b.sb("rsb", [128, 512], F32)
    qf = b.sb("qf", [128, 512], F32); sw = b.sb("sw", [128, 512], F32)
    ktsb = b.sb("ktsb", [128, N], BF16)
    qsb = b.sb("qsb", [128, N], BF16)
    vsb = b.sb("vsb", [128, 32 * 64], BF16)
    csb = b.sb("csb", [128, 512], BF16)
    BCAST(p, std, qwbc, qwr)
    BCAST(p, std, kwbc, kwr)
    for t in range(32):
        ln_prep(b, x, t, nxT, N)

    def qk_post(t, O, nh, gbc, dst):
        v = qf[:, 0:O]
        OP(p, "scalar", "square", out=sw[:, 0:O], in_=v)
        OP(p, "vector", "tensor_reduce", out=std.st[:, 0:nh],
           in_=sw[:, 0:O].rearrange("p (h d) -> p h d", h=nh), axis=AX.X, op=ALU.add)
        OP(p, "scalar", "mul", out=std.st[:, 0:nh], in_=std.st[:, 0:nh], mul=1.0 / 64)
        OP(p, "scalar", "activation", out=std.st[:, 0:nh], in_=std.st[:, 0:nh], func=AF.Ln, bias=CVAL(p, std, 1e-6))
        OP(p, "scalar", "activation", out=std.st[:, 0:nh], in_=std.st[:, 0:nh], func=AF.Exp, scale=-0.5)
        for h in range(nh):
            OP(p, "vector", "tensor_scalar", out=v[:, h * 64:(h + 1) * 64],
               in0=v[:, h * 64:(h + 1) * 64], scalar1=std.st[:, h:h + 1], scalar2=None, op0=ALU.mult)
        OP(p, "vector", "tensor_mul", out=v, in0=v, in1=gbc[:, 0:O])
        p.dma(out=rcb[:, 0:O], in_=rc[t * 128:(t + 1) * 128, 0:O])
        p.dma(out=rsb[:, 0:O], in_=rs[t * 128:(t + 1) * 128, 0:O])
        v4 = v.rearrange("p (h a d) -> p h a d", h=nh, a=2)
        s4 = sw[:, 0:O].rearrange("p (h a d) -> p h a d", h=nh, a=2)
        OP(p, "vector", "tensor_copy", out=s4[:, :, 0:1, :], in_=v4[:, :, 1:2, :])
        OP(p, "vector", "tensor_copy", out=s4[:, :, 1:2, :], in_=v4[:, :, 0:1, :])
        OP(p, "vector", "tensor_mul", out=v, in0=v, in1=rcb[:, 0:O])
        OP(p, "vector", "tensor_mul", out=sw[:, 0:O], in0=sw[:, 0:O], in1=rsb[:, 0:O])
        OP(p, "vector", "tensor_add", out=v, in0=v, in1=sw[:, 0:O])
        OP(p, "vector", "tensor_copy", out=std.tmpE[:, 0:O], in_=v)
        nch = ceil_div(O, 128)
        TRANS(p, [(std.xT[:, c * 128:(c + 1) * 128], std.tmpE[:, c * 128:(c + 1) * 128])
                  for c in range(nch)], std.idb, std.ptb)
        for c in range(nch):
            p.dma(out=dst[c * 128:(c + 1) * 128, t * 128:(t + 1) * 128],
                  in_=std.xT[:, c * 128:(c + 1) * 128])

    WLOAD(p, wsb, wq, [128] * 8, 512, std.wstage)
    for t in range(32):
        xT = [nxT[:, c * N + t * 128:c * N + t * 128 + 128] for c in range(8)]

        def ev(ob, o0, on, ps):
            OP(p, "scalar", "copy", out=qf[:, o0:o0 + on], in_=ps[:, 0:on])
        TLIN(p, xT, wsb, 8, 512, std.ps, ev)
        qk_post(t, 512, 8, qwbc, QT)
    WLOAD(p, wsb, wk, [128] * 8, 128, std.wstage)
    for t in range(32):
        xT = [nxT[:, c * N + t * 128:c * N + t * 128 + 128] for c in range(8)]

        def ev(ob, o0, on, ps):
            OP(p, "scalar", "copy", out=qf[:, o0:o0 + on], in_=ps[:, 0:on])
        TLIN(p, xT, wsb, 8, 128, std.ps, ev)
        qk_post(t, 128, 2, kwbc, KT)
    WLOAD(p, wsb, wv, [128] * 8, 128, std.wstage)
    for t in range(32):
        xT = [nxT[:, c * N + t * 128:c * N + t * 128 + 128] for c in range(8)]

        def ev(ob, o0, on, ps, t=t):
            OP(p, "scalar", "activation", out=std.tmpE[:, 0:on], in_=ps[:, 0:on], func=AF.Copy)
            p.dma(out=V[t * 128:(t + 1) * 128, o0:o0 + on], in_=std.tmpE[:, 0:on])
        TLIN(p, xT, wsb, 8, 128, std.ps, ev)
    heads = [dict(q_off=lh * 64, k_off=((h0 + lh) // 4 - h0 // 4) * 64,
                  v_off=((h0 + lh) // 4 - h0 // 4) * 64, o_off=lh * 64, dh=64, dv=64)
             for lh in range(8)]
    cfg = dict(QT=QT, KT=KT, V=V, CT=ctxo, heads=heads, nq=N, nk=N, nv=N,
               qblocks=[(i * 512, 512) for i in range(8)],
               tiles=lambda hi, qi, q0, qn: [(k * 128, 128, None, [(k * 128, 128)]) for k in range(32)],
               stat_tiles=lambda hi, q0, qn: [(k0, 512, None) for k0 in range(0, N, 512)],
               maxpass=True,
               ktsb=ktsb, qsb=qsb, vsb=vsb, csb=csb)
    ATTN(p, std, cfg)
    if withproj:
        WLOAD(p, wsb, wo, [128] * 8, 1024, std.wstage)
        for t in range(32):
            for c in range(4):
                p.dma(out=std.xT[:, c * 128:(c + 1) * 128],
                      in_=ctxa[c * 128:(c + 1) * 128, t * 128:(t + 1) * 128])
            for c in range(4):
                p.dma(out=std.xT[:, (4 + c) * 128:(5 + c) * 128],
                      in_=ctxo[c * 128:(c + 1) * 128, t * 128:(t + 1) * 128])

            def ev(ob, o0, on, ps):
                OP(p, "scalar", "copy", out=std.tmpF[:, o0:o0 + on], in_=ps[:, 0:on])
            TLIN(p, XTC(std.xT), wsb, 8, 1024, std.ps, ev)
            p.dma(out=so[t * 128:(t + 1) * 128, :], in_=std.tmpF)
    return b.done()



def build_l1f(first, last):
    b = B()
    N = 4096
    HEADS, DH, WIN = 8, 64, 512
    xin = b.inp("xin", [N, 1024])
    wqkv = b.inp("wqkv", [1024, 1536])
    qsr = b.inp("qsr", [1, 512]); ksr = b.inp("ksr", [1, 512])
    wo = b.inp("wo", [512, 1024])
    lngr = b.inp("lngr", [1, 1024]); lnbr = b.inp("lnbr", [1, 1024])
    w1 = b.inp("w1", [1024, 5460]); b1r = b.inp("b1r", [1, 5460])
    w2 = b.inp("w2", [2730, 1024]); b2r = b.inp("b2r", [1, 1024])
    cq = b.inp("cq", [N, 512]); sq = b.inp("sq", [N, 512])
    ck = b.inp("ck", [N, 512]); sk = b.inp("sk", [N, 512])
    mq = b.inp("mq", [4, 128, 512]); mp = b.inp("mp", [4, 128, 512])
    mt = b.inp("mt", [8, 128, 512])
    if last:
        x1 = b.inp("x1", [N, 1024]); s1 = b.inp("s1", [N, 1024])
    yo = b.outp("yo", [N, 1024])
    xc = b.internal("xc", [N, 1024], F32) if first else xin
    xmid = b.internal("xmid", [N, 1024], F32)
    QT = b.internal("QT", [512, N]); KT = b.internal("KT", [512, N])
    V = b.internal("V", [N, 512]); CT = b.internal("CT", [512, N])
    b.begin()
    p, std = b.p, b.std
    import contextlib
    esA = contextlib.ExitStack()
    sbA = lambda n, s, d=BF16: esA.enter_context(b.nc.sbuf_tensor(n, list(s), d))[:]
    wksb = sbA("wksb", [128, 8 * 1536])
    qsbc = sbA("qsbc", [128, 512], F32); ksbc = sbA("ksbc", [128, 512], F32)
    rcb = sbA("rcb", [128, 512], F32); rsb = sbA("rsb", [128, 512], F32)
    qa = sbA("qa", [128, 512], F32); ka = sbA("ka", [128, 512], F32)
    sw = sbA("sw", [128, 512], F32)
    mqs = sbA("mqs", [128, 4 * 512], F32); mps = sbA("mps", [128, 4 * 512], F32)
    mts = sbA("mts", [128, 8 * 512], F32)
    ktsb = sbA("ktsb", [128, N]); qsb2 = sbA("qsb2", [128, N])
    vsb = sbA("vsb", [128, 32 * 64]); csb = sbA("csb", [128, 512])
    wosb = sbA("wosb", [128, 4 * 1024])
    BCAST(p, std, qsbc, qsr)
    BCAST(p, std, ksbc, ksr)
    for j in range(4):
        p.dma(out=mqs[:, j * 512:(j + 1) * 512], in_=mq[j])
        p.dma(out=mps[:, j * 512:(j + 1) * 512], in_=mp[j])
    for j in range(8):
        p.dma(out=mts[:, j * 512:(j + 1) * 512], in_=mt[j])
    if first:
        for t in range(32):
            p.dma(out=std.xt, in_=xin[t * 128:(t + 1) * 128, :])
            LN(p, std.tmpF, std.xt, std.tmpF2, std, 1024)
            p.dma(out=xc[t * 128:(t + 1) * 128, :], in_=std.tmpF)
    WLOAD(p, wksb, wqkv, [128] * 8, 1536, std.wstage)
    WLOAD(p, wosb, wo, [128] * 4, 1024, std.wstage)

    def shift_load(src, t):
        p.dma(out=std.xt[:, 0:512], in_=src[t * 128:(t + 1) * 128, 0:512])
        if t > 0:
            p.dma(out=std.xt[0:128, 512:1024], in_=src[t * 128 - 1:(t + 1) * 128 - 1, 512:1024])
        else:
            OP(p, "vector", "memset", ap=std.xt[0:1, 512:1024], constant=0.0)
            p.dma(out=std.xt[1:128, 512:1024], in_=src[0:127, 512:1024])

    def rope_post(v, ctab, stab, scbc, t, dst):
        OP(p, "scalar", "square", out=sw[:], in_=v)
        OP(p, "vector", "tensor_reduce", out=std.st[:, 0:8],
           in_=sw[:].rearrange("p (h d) -> p h d", h=8), axis=AX.X, op=ALU.add)
        OP(p, "scalar", "sqrt", out=std.st[:, 0:8], in_=std.st[:, 0:8])
        OP(p, "vector", "tensor_scalar", out=std.st[:, 0:8], in0=std.st[:, 0:8],
           scalar1=1e-12, scalar2=None, op0=ALU.max)
        OP(p, "vector", "reciprocal", out=std.st[:, 0:8], in_=std.st[:, 0:8])
        for h in range(8):
            OP(p, "vector", "tensor_scalar", out=v[:, h * 64:(h + 1) * 64],
               in0=v[:, h * 64:(h + 1) * 64], scalar1=std.st[:, h:h + 1], scalar2=None, op0=ALU.mult)
        OP(p, "vector", "tensor_mul", out=v, in0=v, in1=scbc)
        p.dma(out=rcb[:], in_=ctab[t * 128:(t + 1) * 128, :])
        p.dma(out=rsb[:], in_=stab[t * 128:(t + 1) * 128, :])
        v4 = v.rearrange("p (h a d) -> p h a d", h=8, a=2)
        s4 = sw[:].rearrange("p (h a d) -> p h a d", h=8, a=2)
        OP(p, "vector", "tensor_copy", out=s4[:, :, 0:1, :], in_=v4[:, :, 1:2, :])
        OP(p, "vector", "tensor_copy", out=s4[:, :, 1:2, :], in_=v4[:, :, 0:1, :])
        OP(p, "vector", "tensor_mul", out=v, in0=v, in1=rcb[:])
        OP(p, "vector", "tensor_mul", out=sw[:], in0=sw[:], in1=rsb[:])
        OP(p, "vector", "tensor_add", out=v, in0=v, in1=sw[:])
        OP(p, "vector", "tensor_copy", out=std.tmpE[:, 0:512], in_=v)
        TRANS(p, [(std.xT[:, c * 128:(c + 1) * 128], std.tmpE[:, c * 128:(c + 1) * 128])
                  for c in range(4)], std.idb, std.ptb)
        for c in range(4):
            p.dma(out=dst[c * 128:(c + 1) * 128, t * 128:(t + 1) * 128],
                  in_=std.xT[:, c * 128:(c + 1) * 128])

    for t in range(32):
        shift_load(xc, t)
        OP(p, "vector", "tensor_copy", out=std.xb, in_=std.xt)
        TRANS(p, [(std.xT[:, c * 128:(c + 1) * 128], std.xb[:, c * 128:(c + 1) * 128])
                  for c in range(8)], std.idb, std.ptb)
        xT = XTC(std.xT)

        def ev(ob, o0, on, ps, t=t):
            if ob == 0:
                OP(p, "scalar", "copy", out=qa[:], in_=ps[:, 0:on])
            elif ob == 1:
                OP(p, "scalar", "copy", out=ka[:], in_=ps[:, 0:on])
            else:
                OP(p, "scalar", "activation", out=std.tmpE[:, 0:on], in_=ps[:, 0:on], func=AF.Copy)
                p.dma(out=V[t * 128:(t + 1) * 128, 0:on], in_=std.tmpE[:, 0:on])
        TLIN(p, xT, wksb, 8, 1536, std.ps, ev)
        rope_post(qa[:], cq, sq, qsbc, t, QT)
        rope_post(ka[:], ck, sk, ksbc, t, KT)

    heads = [dict(q_off=lh * 64, k_off=lh * 64, v_off=lh * 64, o_off=lh * 64, dh=64, dv=64)
             for lh in range(8)]

    def tiles(hi, qi, q0, qn):
        wi = q0 // 512
        out = []
        for kt in range(8):
            k0g = (wi - 1) * 512 + kt * 128
            if k0g < 0:
                continue
            out.append((k0g, 128, mts[:, kt * 512:(kt + 1) * 512], [(k0g, 128)]))
        return out

    def stat_tiles(hi, q0, qn):
        wi = q0 // 512
        qt = (q0 % 512) // 128
        out = []
        if wi > 0:
            out.append(((wi - 1) * 512, 512, mps[:, qt * 512:(qt + 1) * 512]))
        out.append((wi * 512, 512, mqs[:, qt * 512:(qt + 1) * 512]))
        return out
    cfg = dict(QT=QT, KT=KT, V=V, CT=CT, heads=heads, nq=N, nk=N, nv=N,
               qblocks=[(i * 512, 512) for i in range(8)],
               tiles=tiles, stat_tiles=stat_tiles, maxpass="qb",
               ktsb=ktsb, qsb=qsb2, vsb=vsb, csb=csb)
    ATTN(p, std, cfg)
    for t in range(32):
        for c in range(4):
            p.dma(out=std.xT[:, c * 128:(c + 1) * 128],
                  in_=CT[c * 128:(c + 1) * 128, t * 128:(t + 1) * 128])
        p.dma(out=std.tmpF2, in_=xc[t * 128:(t + 1) * 128, :])

        def ev(ob, o0, on, ps):
            OP(p, "vector", "tensor_tensor", out=std.tmpF[:, o0:o0 + on], in0=ps[:, 0:on],
               in1=std.tmpF2[:, o0:o0 + on], op=ALU.add)
        TLIN(p, XTC(std.xT)[:4], wosb, 4, 1024, std.ps, ev)
        p.dma(out=xmid[t * 128:(t + 1) * 128, :], in_=std.tmpF)
    esA.close()
    esB = contextlib.ExitStack()
    sbB = lambda n, s, d=BF16: esB.enter_context(b.nc.sbuf_tensor(n, list(s), d))[:]
    w1sb = sbB("w1sb", [128, 8 * 5460])
    w2sb = sbB("w2sb", [128, 22 * 1024])
    h1f = sbB("h1f", [128, 5460], F32)
    hT2 = sbB("hT2", [128, 22 * 128])
    b1sb = sbB("b1sb", [1, 5460])
    b2sb = sbB("b2sb", [1, 1024])
    onr = sbB("onr", [1, 128])
    lngbc = sbB("lngbc", [128, 1024], F32)
    lnbbc = sbB("lnbbc", [128, 1024], F32)
    WLOAD(p, w1sb, w1, [128] * 8, 5460, std.wstage)
    WLOAD(p, w2sb, w2, chunks_of(2730), 1024, std.wstage)
    OP(p, "vector", "tensor_copy", out=onr[:], in_=std.on1)
    for o in range(0, 5460, 1024):
        on = min(1024, 5460 - o)
        p.dma(out=std.wstage[0:1, 0:on], in_=b1r[0:1, o:o + on])
        OP(p, "vector", "tensor_copy", out=b1sb[0:1, o:o + on], in_=std.wstage[0:1, 0:on])
    p.dma(out=std.wstage[0:1, :], in_=b2r)
    OP(p, "vector", "tensor_copy", out=b2sb[:], in_=std.wstage[0:1, :])
    BCAST(p, std, lngbc, lngr)
    BCAST(p, std, lnbbc, lnbr)
    cns = chunks_of(2730)
    for t in range(32):
        shift_load(xmid, t)
        LN(p, std.tmpF, std.xt, std.tmpF2, std, 1024)
        OP(p, "vector", "tensor_mul", out=std.tmpF, in0=std.tmpF, in1=lngbc)
        OP(p, "vector", "tensor_add", out=std.tmpF, in0=std.tmpF, in1=lnbbc)
        OP(p, "vector", "tensor_copy", out=std.xb, in_=std.tmpF)
        TRANS(p, [(std.xT[:, c * 128:(c + 1) * 128], std.xb[:, c * 128:(c + 1) * 128])
                  for c in range(8)], std.idb, std.ptb)

        def ev1(ob, o0, on, ps):
            OP(p, "scalar", "copy", out=h1f[:, o0:o0 + on], in_=ps[:, 0:on])
        TLIN2(p, XTC(std.xT), [128] * 8, w1sb, 5460, std.ps, ev1,
              extra=[(onr[:], b1sb[:], 5460)])
        OP(p, "scalar", "activation", out=h1f[:, 2730:5460], in_=h1f[:, 2730:5460], func=AF.Gelu)
        OP(p, "vector", "tensor_mul", out=h1f[:, 0:2730], in0=h1f[:, 0:2730], in1=h1f[:, 2730:5460])
        for c in range(len(cns)):
            cn = cns[c]
            OP(p, "vector", "tensor_copy", out=std.tmpE[:, 0:cn],
               in_=h1f[:, c * 128:c * 128 + cn])
            OP(p, "tensor", "transpose", out=std.ptb[0:cn, 0:128], in_=std.tmpE[:, 0:cn],
               identity=std.idb)
            OP(p, "vector", "tensor_copy", out=hT2[0:cn, c * 128:(c + 1) * 128],
               in_=std.ptb[0:cn, 0:128])
        p.dma(out=std.tmpF2, in_=xmid[t * 128:(t + 1) * 128, :])
        hTl = [hT2[0:cns[c], c * 128:(c + 1) * 128] for c in range(len(cns))]

        def ev2(ob, o0, on, ps):
            OP(p, "vector", "tensor_tensor", out=std.tmpF[:, o0:o0 + on], in0=ps[:, 0:on],
               in1=std.tmpF2[:, o0:o0 + on], op=ALU.add)
        TLIN2(p, hTl, cns, w2sb, 1024, std.ps, ev2,
              extra=[(onr[:], b2sb[:], 1024)])
        if last:
            p.dma(out=std.xt, in_=s1[t * 128:(t + 1) * 128, :])
            OP(p, "scalar", "activation", out=std.tmpF, in_=std.tmpF, func=AF.Silu)
            OP(p, "vector", "tensor_mul", out=std.tmpF, in0=std.tmpF, in1=std.xt)
            p.dma(out=std.xt, in_=x1[t * 128:(t + 1) * 128, :])
            OP(p, "vector", "tensor_add", out=std.tmpF, in0=std.tmpF, in1=std.xt)
        p.dma(out=yo[t * 128:(t + 1) * 128, :], in_=std.tmpF)
    esB.close()
    return b.done()



def build_l2_attn():
    b = B()
    N = 4096
    x = b.inp("x", [N, 1024])
    wq = b.inp("wq", [1024, 1024]); wk = b.inp("wk", [1024, 1024])
    wv = b.inp("wv", [1024, 1024]); wo = b.inp("wo", [1024, 1024])
    bqr = b.inp("bqr", [1, 1024]); bkr = b.inp("bkr", [1, 1024])
    bvr = b.inp("bvr", [1, 1024]); bor = b.inp("bor", [1, 1024])
    cqt = b.inp("cqt", [N, 1024]); sqt = b.inp("sqt", [N, 1024])
    ckt = b.inp("ckt", [N, 1024]); skt = b.inp("skt", [N, 1024])
    mtc = b.inp("mtc", [4, 128, 512])
    so = b.outp("so", [N, 1024])
    QT = b.internal("QT", [1024, N]); KT = b.internal("KT", [1024, N])
    V = b.internal("V", [N, 1024]); CT = b.internal("CT", [1024, N])
    b.begin()
    p, std = b.p, b.std
    nxT = b.sb("nxT", [128, 8 * N], BF16)
    wsb = b.sb("wsb", [128, 8 * 1024], BF16)
    bbc = b.sb("bbc", [128, 1024], F32)
    bobc = b.sb("bobc", [128, 1024], F32)
    rc2 = b.sb("rc2", [128, 1024], F32); rs2 = b.sb("rs2", [128, 1024], F32)
    sw2 = b.sb("sw2", [128, 1024], F32)
    mcs = b.sb("mcs", [128, 4 * 512], F32)
    ktsb = b.sb("ktsb", [128, N], BF16)
    qsb = b.sb("qsb", [128, N], BF16)
    vsb = b.sb("vsb", [128, 32 * 64], BF16)
    csb = b.sb("csb", [128, 512], BF16)
    for j in range(4):
        p.dma(out=mcs[:, j * 512:(j + 1) * 512], in_=mtc[j])
    BCAST(p, std, bobc, bor)
    for t in range(32):
        ln_prep(b, x, t, nxT, N)

    def proj_rope(w, brow, ctab, stab, dst):
        WLOAD(p, wsb, w, [128] * 8, 1024, std.wstage)
        BCAST(p, std, bbc, brow)
        for t in range(32):
            xT = [nxT[:, c * N + t * 128:c * N + t * 128 + 128] for c in range(8)]

            def ev(ob, o0, on, ps):
                OP(p, "vector", "tensor_tensor", out=std.tmpF[:, o0:o0 + on], in0=ps[:, 0:on],
                   in1=bbc[:, o0:o0 + on], op=ALU.add)
            TLIN(p, xT, wsb, 8, 1024, std.ps, ev)
            if ctab is not None:
                p.dma(out=rc2[:], in_=ctab[t * 128:(t + 1) * 128, :])
                p.dma(out=rs2[:], in_=stab[t * 128:(t + 1) * 128, :])
                v4 = std.tmpF.rearrange("p (g a) -> p g a", a=2)
                s4 = sw2.rearrange("p (g a) -> p g a", a=2)
                OP(p, "vector", "tensor_copy", out=s4[:, :, 0:1], in_=v4[:, :, 1:2])
                OP(p, "vector", "tensor_copy", out=s4[:, :, 1:2], in_=v4[:, :, 0:1])
                OP(p, "vector", "tensor_mul", out=std.tmpF, in0=std.tmpF, in1=rc2)
                OP(p, "vector", "tensor_mul", out=sw2, in0=sw2, in1=rs2)
                OP(p, "vector", "tensor_add", out=std.tmpF, in0=std.tmpF, in1=sw2)
            if dst is V:
                OP(p, "vector", "tensor_copy", out=std.xb, in_=std.tmpF)
                p.dma(out=V[t * 128:(t + 1) * 128, :], in_=std.xb)
            else:
                OP(p, "vector", "tensor_copy", out=std.xb, in_=std.tmpF)
                TRANS(p, [(std.xT[:, c * 128:(c + 1) * 128], std.xb[:, c * 128:(c + 1) * 128])
                          for c in range(8)], std.idb, std.ptb)
                for c in range(8):
                    p.dma(out=dst[c * 128:(c + 1) * 128, t * 128:(t + 1) * 128],
                          in_=std.xT[:, c * 128:(c + 1) * 128])
    proj_rope(wq, bqr, cqt, sqt, QT)
    proj_rope(wk, bkr, ckt, skt, KT)
    proj_rope(wv, bvr, None, None, V)
    heads = [dict(q_off=h * 64, k_off=h * 64, v_off=h * 64, o_off=h * 64, dh=64, dv=64)
             for h in range(16)]

    def tiles(hi, qi, q0, qn):
        out = []
        for kc in range(N // 128):
            k0 = kc * 128
            if k0 > q0 + qn - 1:
                break
            r = k0 - q0
            m = mcs[:, (r // 128) * 512:(r // 128) * 512 + 512] if r >= 0 else None
            out.append((k0, 128, m, [(k0, 128)]))
        return out
    cfg = dict(QT=QT, KT=KT, V=V, CT=CT, heads=heads, nq=N, nk=N, nv=N,
               qblocks=[(i * 512, 512) for i in range(8)],
               tiles=tiles, ktsb=ktsb, qsb=qsb, vsb=vsb, csb=csb)
    ATTN(p, std, cfg)
    WLOAD(p, wsb, wo, [128] * 8, 1024, std.wstage)
    for t in range(32):
        for c in range(8):
            p.dma(out=std.xT[:, c * 128:(c + 1) * 128],
                  in_=CT[c * 128:(c + 1) * 128, t * 128:(t + 1) * 128])

        def ev(ob, o0, on, ps):
            OP(p, "vector", "tensor_tensor", out=std.tmpF[:, o0:o0 + on], in0=ps[:, 0:on],
               in1=bobc[:, o0:o0 + on], op=ALU.add)
        TLIN(p, XTC(std.xT), wsb, 8, 1024, std.ps, ev)
        p.dma(out=so[t * 128:(t + 1) * 128, :], in_=std.tmpF)
    return b.done()



def build_l2_ffn():
    b = B()
    T, R = 8, 512
    x = b.inp("x", [T, R, 1024]); s = b.inp("s", [T, R, 1024])
    names = ["wf", "uf", "wi", "ui", "wc", "uc", "wo", "uo"]
    W = {n: b.inp(n, [1024, 1024]) for n in names}
    bfr = b.inp("bfr", [1, 1024]); bir = b.inp("bir", [1, 1024])
    bcr = b.inp("bcr", [1, 1024]); bor = b.inp("bor", [1, 1024])
    incr = b.inp("incr", [1, 1024])
    out = b.outp("out", [T, R, 1024])
    b.begin()
    p, std = b.p, b.std
    w8 = b.sb("w8", [128, 8 * 8 * 1024], BF16)
    for mi, n in enumerate(names):
        WLOAD(p, w8[:, mi * 8192:(mi + 1) * 8192], W[n], [128] * 8, 1024, std.wstage)
    onr = b.sb("onr", [1, 128], BF16)
    OP(p, "vector", "tensor_copy", out=onr[:], in_=std.on1)
    bsb = {}
    for n, r in [("f", bfr), ("i", bir), ("c", bcr), ("o", bor)]:
        bsb[n] = b.sb("bs" + n, [1, 1024], BF16)
        p.dma(out=std.wstage[0:1, :], in_=r)
        OP(p, "vector", "tensor_copy", out=bsb[n][:], in_=std.wstage[0:1, :])
    cst = b.sb("cst", [128, 4 * 1024], F32)
    cT = b.sb("cT", [128, 8 * 128], BF16)
    gf = b.sb("gf", [128, 1024], F32); gi = b.sb("gi", [128, 1024], F32)
    gg = b.sb("gg", [128, 1024], F32)
    BCAST(p, std, std.tmpF, incr)
    for j in range(4):
        OP(p, "vector", "tensor_copy", out=cst[:, j * 1024:(j + 1) * 1024], in_=std.tmpF)

    def wi_(mi):
        return w8[:, mi * 8192:(mi + 1) * 8192]

    def gate(xT, cTl, wm, um, bn, func, dst):
        for oc in range(2):
            o0 = oc * 512
            ps = std.ps[oc % 2]
            mm = []
            for c in range(8):
                mm.append(dict(out=ps[:, 0:512], lhsT=xT[c], rhs=wi_(wm)[:, c * 1024 + o0:c * 1024 + o0 + 512],
                               start=(c == 0), stop=False))
            for c in range(8):
                mm.append(dict(out=ps[:, 0:512], lhsT=cTl[c], rhs=wi_(um)[:, c * 1024 + o0:c * 1024 + o0 + 512],
                               start=False, stop=False))
            mm.append(dict(out=ps[:, 0:512], lhsT=onr[:], rhs=bsb[bn][0:1, o0:o0 + 512],
                           start=False, stop=True))
            MMS(p, mm)
            OP(p, "scalar", "activation", out=dst[:, o0:o0 + 512], in_=ps[:, 0:512],
               func=func)

    for t in range(T):
        for j in range(4):
            cj = cst[:, j * 1024:(j + 1) * 1024]
            p.dma(out=std.xt, in_=x[t, j * 128:(j + 1) * 128, :])
            LN(p, std.tmpF, std.xt, std.tmpF2, std, 1024)
            OP(p, "vector", "tensor_copy", out=std.xb, in_=std.tmpF)
            TRANS(p, [(std.xT[:, c * 128:(c + 1) * 128], std.xb[:, c * 128:(c + 1) * 128])
                      for c in range(8)], std.idb, std.ptb)
            xT = XTC(std.xT)
            OP(p, "vector", "tensor_copy", out=std.xb, in_=cj)
            TRANS(p, [(cT[:, c * 128:(c + 1) * 128], std.xb[:, c * 128:(c + 1) * 128])
                      for c in range(8)], std.idb, std.ptb)
            cTl = XTC(cT)
            gate(xT, cTl, 0, 1, "f", AF.Sigmoid, gf)
            gate(xT, cTl, 2, 3, "i", AF.Sigmoid, gi)
            gate(xT, cTl, 4, 5, "c", AF.Tanh, gg)
            OP(p, "vector", "tensor_mul", out=cj, in0=gf, in1=cj)
            OP(p, "vector", "tensor_mul", out=gg, in0=gi, in1=gg)
            OP(p, "vector", "tensor_add", out=cj, in0=cj, in1=gg)
    for t in range(T):
        for j in range(4):
            p.dma(out=std.xt, in_=x[t, j * 128:(j + 1) * 128, :])
            LN(p, std.tmpF, std.xt, std.tmpF2, std, 1024)
            OP(p, "vector", "tensor_copy", out=std.xb, in_=std.tmpF)
            TRANS(p, [(std.xT[:, c * 128:(c + 1) * 128], std.xb[:, c * 128:(c + 1) * 128])
                      for c in range(8)], std.idb, std.ptb)
            OP(p, "vector", "tensor_copy", out=std.xb, in_=cst[:, j * 1024:(j + 1) * 1024])
            TRANS(p, [(cT[:, c * 128:(c + 1) * 128], std.xb[:, c * 128:(c + 1) * 128])
                      for c in range(8)], std.idb, std.ptb)
            cTl = XTC(cT)
            gate(XTC(std.xT), cTl, 6, 7, "o", AF.Tanh, gf)
            OP(p, "scalar", "activation", out=gf[:], in_=gf[:], func=AF.Silu)
            p.dma(out=std.tmpF, in_=s[t, j * 128:(j + 1) * 128, :])
            OP(p, "vector", "tensor_mul", out=gf[:], in0=gf[:], in1=std.tmpF)
            OP(p, "vector", "tensor_add", out=gf[:], in0=gf[:], in1=std.xt)
            p.dma(out=out[t, j * 128:(j + 1) * 128, :], in_=gf[:])
    return b.done()



def build_l3_attn():
    b = B()
    T, R, H, DK, DV = 8, 512, 16, 32, 64
    x = b.inp("x", [T, R, 1024])
    convr = b.inp("convr", [4, 1024])
    wq = b.inp("wq", [1024, 512]); wkg = b.inp("wkg", [1024, 512])
    wv = b.inp("wv", [1024, 1024]); wg = b.inp("wg", [1024, 1024])
    bgr = b.inp("bgr", [1, 1024]); wout = b.inp("wout", [1024, 1024])
    augr = b.inp("augr", [1, 512])
    so = b.outp("so", [T, R, 1024])
    b.begin()
    p, std = b.p, b.std
    wsb_q = b.sb("wsb_q", [128, 8 * 512], BF16)
    wsb_kg = b.sb("wsb_kg", [128, 8 * 512], BF16)
    wsb_v = b.sb("wsb_v", [128, 8 * 1024], BF16)
    wsb_g = b.sb("wsb_g", [128, 8 * 1024], BF16)
    wsb_o = b.sb("wsb_o", [128, 8 * 1024], BF16)
    WLOAD(p, wsb_q, wq, [128] * 8, 512, std.wstage)
    WLOAD(p, wsb_kg, wkg, [128] * 8, 512, std.wstage)
    WLOAD(p, wsb_v, wv, [128] * 8, 1024, std.wstage)
    WLOAD(p, wsb_g, wg, [128] * 8, 1024, std.wstage)
    WLOAD(p, wsb_o, wout, [128] * 8, 1024, std.wstage)
    cbc = b.sb("cbc", [128, 4 * 1024], F32)
    for k in range(4):
        BCAST(p, std, cbc[:, k * 1024:(k + 1) * 1024], convr[k:k + 1, :])
    bgbc = b.sb("bgbc", [128, 1024], F32)
    BCAST(p, std, bgbc, bgr)
    augbc = b.sb("augbc", [128, 512], F32)
    BCAST(p, std, augbc, augr)
    xnring = b.sb("xnring", [128, 4 * 1024], F32)
    vh = b.sb("vh", [128, 8 * 1024], F32)
    gh = b.sb("gh", [128, 1024], F32)
    bh = b.sb("bh", [128, 8 * 512], F32)
    kh = b.sb("kh", [128, 512], F32)
    qh = b.sb("qh", [128, 512], F32)
    Gr = b.sb("Gr", [128, 512], F32)
    ah = b.sb("ah", [128, 512], F32)
    of = b.sb("of", [128, 1024], F32)
    a16 = b.sb("a16", [128, 16], F32)
    C0 = math.log(DK ** -0.5)
    for j in range(4):
        for l in range(T):
            slot = xnring[:, (l % 4) * 1024:(l % 4 + 1) * 1024]
            p.dma(out=std.xt, in_=x[l, j * 128:(j + 1) * 128, :])
            LN(p, slot, std.xt, std.tmpF2, std, 1024)
            first = True
            for k in range(4):
                ls = l - 3 + k
                if ls < 0:
                    continue
                ssl = xnring[:, (ls % 4) * 1024:(ls % 4 + 1) * 1024]
                if first:
                    OP(p, "vector", "tensor_mul", out=std.tmpF, in0=ssl, in1=cbc[:, k * 1024:(k + 1) * 1024])
                    first = False
                else:
                    OP(p, "vector", "tensor_mul", out=std.tmpF2, in0=ssl, in1=cbc[:, k * 1024:(k + 1) * 1024])
                    OP(p, "vector", "tensor_add", out=std.tmpF, in0=std.tmpF, in1=std.tmpF2)
            OP(p, "scalar", "activation", out=std.tmpF, in_=std.tmpF, func=AF.Silu)
            OP(p, "vector", "tensor_copy", out=std.xb, in_=std.tmpF)
            TRANS(p, [(std.xT[:, c * 128:(c + 1) * 128], std.xb[:, c * 128:(c + 1) * 128])
                      for c in range(8)], std.idb, std.ptb)
            xT = XTC(std.xT)

            def evq(ob, o0, on, ps):
                OP(p, "scalar", "copy", out=qh[:, o0:o0 + on], in_=ps[:, 0:on])
            TLIN(p, xT, wsb_q, 8, 512, std.ps, evq)

            def evkg(ob, o0, on, ps):
                OP(p, "scalar", "copy", out=kh[:, o0:o0 + on], in_=ps[:, 0:on])
            TLIN(p, xT, wsb_kg, 8, 512, std.ps, evkg)

            def evv(ob, o0, on, ps, l=l):
                OP(p, "scalar", "copy", out=vh[:, l * 1024 + o0:l * 1024 + o0 + on], in_=ps[:, 0:on])
            TLIN(p, xT, wsb_v, 8, 1024, std.ps, evv)

            def evg(ob, o0, on, ps):
                OP(p, "vector", "tensor_tensor", out=gh[:, o0:o0 + on], in0=ps[:, 0:on],
                   in1=bgbc[:, o0:o0 + on], op=ALU.add)
            TLIN(p, xT, wsb_g, 8, 1024, std.ps, evg)
            OP(p, "scalar", "activation", out=std.tmpF[:, 0:512], in_=kh[:], func=AF.Abs)
            OP(p, "scalar", "activation", out=std.tmpF[:, 0:512], in_=std.tmpF[:, 0:512],
               func=AF.Exp, scale=-1.0)
            OP(p, "scalar", "activation", out=std.tmpF[:, 0:512], in_=std.tmpF[:, 0:512],
               func=AF.Ln, bias=CVAL(p, std, 1.0))
            OP(p, "vector", "tensor_scalar", out=std.tmpF2[:, 0:512], in0=kh[:],
               scalar1=0.0, scalar2=None, op0=ALU.min)
            OP(p, "vector", "tensor_sub", out=std.tmpF2[:, 0:512], in0=std.tmpF2[:, 0:512],
               in1=std.tmpF[:, 0:512])
            OP(p, "scalar", "mul", out=std.tmpF2[:, 0:512], in_=std.tmpF2[:, 0:512], mul=1.0 / 16.0)
            gk = std.tmpF2[:, 0:512]
            OP(p, "scalar", "activation", out=kh[:], in_=gk, func=AF.Exp)
            OP(p, "scalar", "mul", out=kh[:], in_=kh[:], mul=-1.0)
            OP(p, "scalar", "add", out=kh[:], in_=kh[:], add=1.0)
            if l == 0:
                OP(p, "vector", "tensor_copy", out=Gr[:], in_=gk)
            else:
                OP(p, "vector", "tensor_add", out=Gr[:], in0=Gr[:], in1=gk)
            OP(p, "scalar", "activation", out=std.tmpF[:, 0:512], in_=Gr[:], func=AF.Exp, bias=CVAL(p, std, C0))
            OP(p, "vector", "tensor_mul", out=ah[:], in0=qh[:], in1=std.tmpF[:, 0:512])
            OP(p, "scalar", "activation", out=std.tmpF[:, 0:512], in_=Gr[:], func=AF.Exp, scale=-1.0)
            OP(p, "vector", "tensor_mul", out=bh[:, l * 512:(l + 1) * 512], in0=kh[:],
               in1=std.tmpF[:, 0:512])
            for sidx in range(l + 1):
                OP(p, "vector", "tensor_mul", out=std.tmpF[:, 0:512], in0=ah[:],
                   in1=bh[:, sidx * 512:(sidx + 1) * 512])
                OP(p, "vector", "tensor_reduce", out=a16[:],
                   in_=std.tmpF[:, 0:512].rearrange("p (h d) -> p h d", h=16), axis=AX.X, op=ALU.add)
                for h in range(16):
                    OP(p, "vector", "tensor_scalar", out=std.tmpF2[:, h * 64:(h + 1) * 64],
                       in0=vh[:, sidx * 1024 + h * 64:sidx * 1024 + (h + 1) * 64],
                       scalar1=a16[:, h:h + 1], scalar2=None, op0=ALU.mult)
                if sidx == 0:
                    OP(p, "vector", "tensor_copy", out=of[:], in_=std.tmpF2)
                else:
                    OP(p, "vector", "tensor_add", out=of[:], in0=of[:], in1=std.tmpF2)
            OP(p, "vector", "tensor_mul", out=std.tmpF[:, 0:512], in0=qh[:], in1=kh[:])
            OP(p, "vector", "tensor_mul", out=std.tmpF[:, 0:512], in0=std.tmpF[:, 0:512], in1=augbc)
            OP(p, "vector", "tensor_reduce", out=a16[:],
               in_=std.tmpF[:, 0:512].rearrange("p (h d) -> p h d", h=16), axis=AX.X, op=ALU.add)
            for h in range(16):
                OP(p, "vector", "tensor_scalar", out=std.tmpF2[:, h * 64:(h + 1) * 64],
                   in0=vh[:, l * 1024 + h * 64:l * 1024 + (h + 1) * 64],
                   scalar1=a16[:, h:h + 1], scalar2=None, op0=ALU.mult)
            OP(p, "scalar", "activation", out=std.tmpF2, in_=std.tmpF2, func=AF.Sigmoid)
            OP(p, "vector", "tensor_add", out=of[:], in0=of[:], in1=std.tmpF2)
            OP(p, "vector", "tensor_reduce", out=a16[:],
               in_=of[:].rearrange("p (h d) -> p h d", h=16), axis=AX.X, op=ALU.add)
            OP(p, "scalar", "mul", out=a16[:], in_=a16[:], mul=1.0 / 64)
            OP(p, "scalar", "square", out=std.tmpF, in_=of[:])
            OP(p, "vector", "tensor_reduce", out=std.st[:, 0:16],
               in_=std.tmpF.rearrange("p (h d) -> p h d", h=16), axis=AX.X, op=ALU.add)
            OP(p, "scalar", "mul", out=std.st[:, 0:16], in_=std.st[:, 0:16], mul=1.0 / 64)
            OP(p, "vector", "tensor_mul", out=std.tmpF[:, 0:16], in0=a16[:], in1=a16[:])
            OP(p, "vector", "tensor_sub", out=std.st[:, 0:16], in0=std.st[:, 0:16],
               in1=std.tmpF[:, 0:16])
            OP(p, "scalar", "activation", out=std.st[:, 0:16], in_=std.st[:, 0:16],
               func=AF.Ln, bias=CVAL(p, std, 1e-5))
            OP(p, "scalar", "activation", out=std.st[:, 0:16], in_=std.st[:, 0:16],
               func=AF.Exp, scale=-0.5)
            for h in range(16):
                OP(p, "vector", "tensor_scalar", out=of[:, h * 64:(h + 1) * 64],
                   in0=of[:, h * 64:(h + 1) * 64], scalar1=a16[:, h:h + 1], scalar2=None,
                   op0=ALU.subtract)
                OP(p, "vector", "tensor_scalar", out=of[:, h * 64:(h + 1) * 64],
                   in0=of[:, h * 64:(h + 1) * 64], scalar1=std.st[:, h:h + 1], scalar2=None,
                   op0=ALU.mult)
            OP(p, "scalar", "activation", out=gh[:], in_=gh[:], func=AF.Silu)
            OP(p, "vector", "tensor_mul", out=of[:], in0=of[:], in1=gh[:])
            OP(p, "vector", "tensor_copy", out=std.xb, in_=of[:])
            TRANS(p, [(std.xT[:, c * 128:(c + 1) * 128], std.xb[:, c * 128:(c + 1) * 128])
                      for c in range(8)], std.idb, std.ptb)

            def evo(ob, o0, on, ps):
                OP(p, "scalar", "copy", out=std.tmpF[:, o0:o0 + on], in_=ps[:, 0:on])
            TLIN(p, XTC(std.xT), wsb_o, 8, 1024, std.ps, evo)
            p.dma(out=so[l, j * 128:(j + 1) * 128, :], in_=std.tmpF)
    return b.done()



def build_l3f(part):
    b = B()
    N = 4096
    x = b.inp("x", [N, 1024])
    wgr = b.inp("wgr", [1024, 16])
    ew1 = b.inp("ew1", [8, 1024, 1024]); ew2 = b.inp("ew2", [8, 512, 1024])
    if part == 0:
        z = b.outp("z", [N, 1024])
    else:
        ws1 = b.inp("ws1", [1024, 2048]); ws2 = b.inp("ws2", [1024, 1024])
        z = b.inp("z", [N, 1024])
        s3 = b.inp("s3", [N, 1024])
        zo = b.outp("zo", [N, 1024])
        xo = b.outp("xo", [N, 1024])
    b.begin()
    p, std = b.p, b.std
    nxT = b.sb("nxT", [128, 8 * N], BF16)
    w1sb = b.sb("w1sb", [128, 8 * 1024], BF16)
    w2sb = b.sb("w2sb", [128, 4 * 1024], BF16)
    cw = b.sb("cw", [128, 32 * 16], F32)
    r16 = b.sb("r16", [128, 16], F32); r16b = b.sb("r16b", [128, 16], F32)
    r16c = b.sb("r16c", [128, 16], F32)
    hb = b.sb("hb", [128, 512], BF16)
    zt = b.sb("zt", [128, 1024], F32)
    wgsb = b.sb("wgsb", [128, 8 * 16], BF16)
    WLOAD(p, wgsb, wgr, [128] * 8, 16, std.wstage)
    for t in range(32):
        ln_prep(b, x, t, nxT, N)
        xT = [nxT[:, c * N + t * 128:c * N + t * 128 + 128] for c in range(8)]
        MMS(p, [dict(out=std.ps[0][:, 0:16], lhsT=xT[c], rhs=wgsb[:, c * 16:(c + 1) * 16],
                     start=(c == 0), stop=(c == 7)) for c in range(8)])
        OP(p, "vector", "tensor_reduce", out=std.st[:, 0:1], in_=std.ps[0][:, 0:16],
           axis=AX.X, op=ALU.max)
        OP(p, "scalar", "mul", out=std.st[:, 1:2], in_=std.st[:, 0:1], mul=-1.0)
        OP(p, "scalar", "activation", out=r16[:], in_=std.ps[0][:, 0:16], func=AF.Exp,
           bias=std.st[:, 1:2])
        OP(p, "vector", "tensor_reduce", out=std.st[:, 2:3], in_=r16[:], axis=AX.X, op=ALU.add)
        OP(p, "vector", "reciprocal", out=std.st[:, 3:4], in_=std.st[:, 2:3])
        OP(p, "vector", "tensor_scalar", out=r16[:], in0=r16[:], scalar1=std.st[:, 3:4],
           scalar2=None, op0=ALU.mult)
        OP(p, "vector", "tensor_reduce", out=std.st[:, 4:5], in_=r16[:], axis=AX.X, op=ALU.max)
        OP(p, "vector", "tensor_scalar", out=r16b[:], in0=r16[:], scalar1=std.st[:, 4:5],
           scalar2=None, op0=ALU.is_ge)
        OP(p, "vector", "scalar_tensor_tensor", out=r16c[:], in0=r16b[:], scalar=-1e9,
           in1=r16[:], op0=ALU.mult, op1=ALU.add)
        OP(p, "vector", "tensor_reduce", out=std.st[:, 5:6], in_=r16c[:], axis=AX.X, op=ALU.max)
        OP(p, "vector", "tensor_scalar", out=r16b[:], in0=r16[:], scalar1=std.st[:, 5:6],
           scalar2=None, op0=ALU.is_ge)
        OP(p, "vector", "tensor_mul", out=cw[:, t * 16:(t + 1) * 16], in0=r16[:], in1=r16b[:])
    for e in range(8):
        ge = part * 8 + e
        WLOAD(p, w1sb, ew1[e], [128] * 8, 1024, std.wstage)
        WLOAD(p, w2sb, ew2[e], [128] * 4, 1024, std.wstage)
        for t in range(32):
            xT = [nxT[:, c * N + t * 128:c * N + t * 128 + 128] for c in range(8)]

            def ev1(ob, o0, on, ps):
                OP(p, "scalar", "copy", out=std.tmpF[:, o0:o0 + on], in_=ps[:, 0:on])
            TLIN(p, xT, w1sb, 8, 1024, std.ps, ev1)
            OP(p, "scalar", "activation", out=std.tmpF[:, 512:1024], in_=std.tmpF[:, 512:1024],
               func=AF.Silu)
            OP(p, "vector", "tensor_mul", out=hb[:], in0=std.tmpF[:, 0:512], in1=std.tmpF[:, 512:1024])
            TRANS(p, [(std.xT[:, c * 128:(c + 1) * 128], hb[:, c * 128:(c + 1) * 128])
                      for c in range(4)], std.idb, std.ptb)

            def ev2(ob, o0, on, ps, t=t, ge=ge):
                OP(p, "vector", "tensor_scalar", out=std.tmpF[:, o0:o0 + on], in0=ps[:, 0:on],
                   scalar1=cw[:, t * 16 + ge:t * 16 + ge + 1], scalar2=None, op0=ALU.mult)
            TLIN(p, XTC(std.xT)[:4], w2sb, 4, 1024, std.ps, ev2)
            if part == 0 and e == 0:
                p.dma(out=z[t * 128:(t + 1) * 128, :], in_=std.tmpF)
            else:
                zsrc = z if (part == 0 or e == 0) else zo
                zdst = z if part == 0 else zo
                p.dma(out=zt[:], in_=zsrc[t * 128:(t + 1) * 128, :])
                OP(p, "vector", "tensor_add", out=std.tmpF[:], in0=std.tmpF[:], in1=zt[:])
                p.dma(out=zdst[t * 128:(t + 1) * 128, :], in_=std.tmpF)
    if part == 1:
        ws1sb = b.sb("ws1sb", [128, 8 * 2048], BF16)
        ws2sb = b.sb("ws2sb", [128, 8 * 1024], BF16)
        WLOAD(p, ws1sb, ws1, [128] * 8, 2048, std.wstage)
        WLOAD(p, ws2sb, ws2, [128] * 8, 1024, std.wstage)
        hb2 = b.sb("hb2", [128, 1024], BF16)
        for t in range(32):
            xT = [nxT[:, c * N + t * 128:c * N + t * 128 + 128] for c in range(8)]

            def ev1(ob, o0, on, ps, t=t):
                OP(p, "scalar", "copy", out=std.tmpF2[:, 0:1024][:, o0 - 1024:o0 - 1024 + on]
                   if o0 >= 1024 else std.tmpF[:, o0:o0 + on], in_=ps[:, 0:on])
            TLIN(p, xT, ws1sb, 8, 2048, std.ps, ev1)
            OP(p, "scalar", "activation", out=std.tmpF2, in_=std.tmpF2, func=AF.Silu)
            OP(p, "vector", "tensor_mul", out=hb2[:], in0=std.tmpF, in1=std.tmpF2)
            TRANS(p, [(std.xT[:, c * 128:(c + 1) * 128], hb2[:, c * 128:(c + 1) * 128])
                      for c in range(8)], std.idb, std.ptb)

            def ev2(ob, o0, on, ps):
                OP(p, "scalar", "copy", out=std.tmpF[:, o0:o0 + on], in_=ps[:, 0:on])
            TLIN(p, XTC(std.xT), ws2sb, 8, 1024, std.ps, ev2)
            p.dma(out=zt[:], in_=zo[t * 128:(t + 1) * 128, :])
            OP(p, "vector", "tensor_add", out=zt[:], in0=zt[:], in1=std.tmpF)
            OP(p, "scalar", "activation", out=zt[:], in_=zt[:], func=AF.Silu)
            p.dma(out=std.tmpF, in_=s3[t * 128:(t + 1) * 128, :])
            OP(p, "vector", "tensor_mul", out=zt[:], in0=zt[:], in1=std.tmpF)
            p.dma(out=std.tmpF2, in_=x[t * 128:(t + 1) * 128, :])
            OP(p, "vector", "tensor_add", out=zt[:], in0=zt[:], in1=std.tmpF2)
            p.dma(out=xo[t * 128:(t + 1) * 128, :], in_=zt[:])
    return b.done()



def _colchunks(v):
    return np.ascontiguousarray(np.asarray(v, np.float32).reshape(8, 128).T)


def _rope_l1():
    n = 4096
    inv = 1.0 / 10000.0 ** (np.arange(0, 64, 2, dtype=np.float64) / 64)
    pos = np.arange(n - 1, dtype=np.float64)
    f = pos[:, None] * inv[None, :]
    emb = np.concatenate([f, f], 1)
    cos = np.cos(emb); sin = np.sin(emb)
    C = np.vstack([np.ones((1, 64)), cos])
    S = np.vstack([np.zeros((1, 64)), sin])
    Sp = S.copy(); Sp[:, 0:32] *= -1.0
    return (np.tile(C, (1, 8)).astype(np.float32), np.tile(Sp, (1, 8)).astype(np.float32))


def _rope_l2():
    n = 4096
    inv = 1.0 / 10000.0 ** (np.arange(0, 64, 2, dtype=np.float64) / 64)
    ang = np.arange(n, dtype=np.float64)[:, None] * inv[None, :]
    c = np.cos(ang); s = np.sin(ang)
    C64 = np.zeros((n, 64)); S64 = np.zeros((n, 64))
    C64[:, 0::2] = c; C64[:, 1::2] = c
    S64[:, 0::2] = -s; S64[:, 1::2] = s
    Cq = np.tile(C64 * 0.125, (1, 16)).astype(np.float32)
    Sq = np.tile(S64 * 0.125, (1, 16)).astype(np.float32)
    Ck = np.tile(C64, (1, 16)).astype(np.float32)
    Sk = np.tile(S64, (1, 16)).astype(np.float32)
    return Cq, Sq, Ck, Sk


def _xpos_l1f():
    n = 4096
    hi = np.arange(0, 64, 2, dtype=np.float64)
    inv = 1.0 / 10000.0 ** (hi / 64)
    t = np.arange(n, dtype=np.float64)
    f = t[:, None] * inv[None, :]
    freqs = np.concatenate([f, f], 1)
    base = (hi + 0.4 * 64) / (1.4 * 64)
    power = (t - n // 2) / 256.0
    scale = base[None, :] ** power[:, None]
    scale = np.concatenate([scale, scale], 1)
    cos = np.cos(freqs); sin = np.sin(freqs)
    sgn = np.ones((1, 64)); sgn[0, 0:32] = -1.0
    cq = cos * scale; sq = sin * scale * sgn
    ck = cos / scale; sk = sin / scale * sgn
    T8 = lambda a: np.tile(a, (1, 8)).astype(np.float32)
    return T8(cq), T8(sq), T8(ck), T8(sk)


def _mask_l0_local():
    m = np.full((5, 128, 512), NEG, np.float32)
    rows = np.arange(128); cols = np.arange(512)
    for j in range(5):
        kb = 2 * j - 1 + (rows >= 64).astype(int)
        qb = cols // 64
        valid = np.abs(kb[:, None] - qb[None, :]) <= 1
        m[j][valid] = 0.0
    return m


def _masks_l1f():
    mq = np.full((4, 128, 512), NEG, np.float32)
    mp = np.full((4, 128, 512), NEG, np.float32)
    mt = np.full((8, 128, 512), NEG, np.float32)
    for qt in range(4):
        i = qt * 128 + np.arange(128)
        j = np.arange(512)
        mq[qt][j[None, :] <= i[:, None]] = 0.0
        mp[qt][j[None, :] >= i[:, None]] = 0.0
    for kt in range(8):
        kl = kt * 128 + np.arange(128)
        i = np.arange(512)
        mt[kt][(kl[:, None] >= i[None, :]) & (kl[:, None] <= i[None, :] + 512)] = 0.0
    return mq, mp, mt


def _mask_l2():
    m = np.full((4, 128, 512), NEG, np.float32)
    for r in range(4):
        k = r * 128 + np.arange(128)
        q = np.arange(512)
        m[r][k[:, None] <= q[None, :]] = 0.0
    return m


def RUN(nc, percore, shared):
    consts = _consts()
    maps = [dict(shared, **percore[c], **consts) for c in range(NCORE)]
    res = run_bass_kernel_spmd(nc, maps, core_ids=list(range(NCORE)))
    return res.results


def kernel(**inp):
    f = lambda k: np.ascontiguousarray(np.asarray(inp[k], np.float32))
    x0 = f('L0_attn_x')
    gi0 = int(np.asarray(inp['L0_attn_global_indices'])[0])
    gi1 = int(np.asarray(inp['L0_attn_global_indices'])[1])

    nc = _cached(('l0a', gi0, gi1), lambda: build_l0_attn(gi0, gi1))
    sh = dict(wq=hT(f('L0_attn_Wq')), wk=hT(f('L0_attn_Wk')), wv=hT(f('L0_attn_Wv')),
              wo=hT(f('L0_attn_Wo')), bqc=_colchunks(f('L0_attn_bq')), bkc=_colchunks(f('L0_attn_bk')),
              bvr=hrow(f('L0_attn_bv')), bor=hrow(f('L0_attn_bo')), mkl=_mask_l0_local())
    res = RUN(nc, [{'x': x0[c]} for c in range(NCORE)], sh)
    s0 = np.stack([np.asarray(res[c]['so'], np.float32) for c in range(NCORE)])

    nc = _cached('l0fa', build_l0f_a)
    sh = dict(wq=hT(f('L0_ffn_Wq')), wkc=hT(f('L0_ffn_Wkc')), wvc=hT(f('L0_ffn_Wvc')),
              wo=hT(f('L0_ffn_Wo')), g1=hrow(f('L0_ffn_g1')))
    res = RUN(nc, [{'x': x0[c]} for c in range(NCORE)], sh)
    xa = np.stack([np.asarray(res[c]['xa'], np.float32) for c in range(NCORE)])

    ew1 = np.ascontiguousarray(np.transpose(f('L0_ffn_eW1'), (0, 2, 1)))
    ew3 = np.ascontiguousarray(np.transpose(f('L0_ffn_eW3'), (0, 2, 1)))
    ew2 = np.ascontiguousarray(np.transpose(f('L0_ffn_eW2'), (0, 2, 1)))
    shb = dict(g2=hrow(f('L0_ffn_g2')), wr=hT(f('L0_ffn_Wr')), ebr=hrow(f('L0_ffn_expert_bias')))
    nc = _cached('l0fb0', lambda: build_l0f_b(0))
    sh = dict(shb, sw1=hT(f('L0_ffn_sW1')), sw2=hT(f('L0_ffn_sW2')), sw3=hT(f('L0_ffn_sW3')),
              ew1=ew1[0:4], ew3=ew3[0:4], ew2=ew2[0:4])
    res = RUN(nc, [{'xa': xa[c]} for c in range(NCORE)], sh)
    z0 = np.stack([np.asarray(res[c]['z'], np.float32) for c in range(NCORE)])
    nc = _cached('l0fb1', lambda: build_l0f_b(1))
    sh = dict(shb, ew1=ew1[4:8], ew3=ew3[4:8], ew2=ew2[4:8])
    res = RUN(nc, [{'xa': xa[c], 'z': z0[c], 'x0': x0[c], 's0': s0[c]} for c in range(NCORE)], sh)
    x1 = np.stack([np.asarray(res[c]['xo'], np.float32) for c in range(NCORE)])

    rc1, rs1 = _rope_l1()
    wqT = hT(f('L1_attn_Wq')); wkT = hT(f('L1_attn_Wk')); wvT = hT(f('L1_attn_Wv'))
    qw = np.tile(f('L1_attn_q_norm_w'), 8)[None, :]
    kw = np.tile(f('L1_attn_k_norm_w'), 2)[None, :]
    nc = _cached('l1a0', lambda: build_l1_attn(0, False))
    sh = dict(wq=np.ascontiguousarray(wqT[:, 0:512]), wk=np.ascontiguousarray(wkT[:, 0:128]),
              wv=np.ascontiguousarray(wvT[:, 0:128]), qwr=np.ascontiguousarray(qw),
              kwr=np.ascontiguousarray(kw), rc=rc1, rs=rs1)
    res = RUN(nc, [{'x': x1[c]} for c in range(NCORE)], sh)
    ctxa = [res[c]['ctxo'] for c in range(NCORE)]
    nc = _cached('l1a1', lambda: build_l1_attn(8, True))
    sh = dict(wq=np.ascontiguousarray(wqT[:, 512:1024]), wk=np.ascontiguousarray(wkT[:, 128:256]),
              wv=np.ascontiguousarray(wvT[:, 128:256]), qwr=np.ascontiguousarray(qw),
              kwr=np.ascontiguousarray(kw), rc=rc1, rs=rs1, wo=hT(f('L1_attn_Wo')))
    res = RUN(nc, [{'x': x1[c], 'ctxa': ctxa[c]} for c in range(NCORE)], sh)
    s1 = np.stack([np.asarray(res[c]['so'], np.float32) for c in range(NCORE)])

    cq, sq, ck, sk = _xpos_l1f()
    mq, mp, mt = _masks_l1f()
    cur = x1
    for l in range(2):
        first = (l == 0)
        last = (l == 1)
        nc = _cached(('l1f', l), lambda l=l: build_l1f(l == 0, l == 1))
        sh = dict(wqkv=f('L1_ffn_wqkv')[l], qsr=np.tile(f('L1_ffn_q_scale')[l], 8)[None, :],
                  ksr=np.tile(f('L1_ffn_k_scale')[l], 8)[None, :], wo=f('L1_ffn_wo')[l],
                  lngr=hrow(f('L1_ffn_ln_g')[l]), lnbr=hrow(f('L1_ffn_ln_b')[l]),
                  w1=f('L1_ffn_w1')[l], b1r=hrow(f('L1_ffn_b1')[l]),
                  w2=f('L1_ffn_w2')[l], b2r=hrow(f('L1_ffn_b2')[l]),
                  cq=cq, sq=sq, ck=ck, sk=sk, mq=mq, mp=mp, mt=mt)
        sh = {k: np.ascontiguousarray(v) for k, v in sh.items()}
        if last:
            res = RUN(nc, [{'xin': cur[c], 'x1': x1[c], 's1': s1[c]} for c in range(NCORE)], sh)
        else:
            res = RUN(nc, [{'xin': cur[c]} for c in range(NCORE)], sh)
        cur = np.stack([np.asarray(res[c]['yo'], np.float32) for c in range(NCORE)])
    x2 = cur

    cq2, sq2, ck2, sk2 = _rope_l2()
    nc = _cached('l2a', build_l2_attn)
    sh = dict(wq=hT(f('L2_attn_Wq')), wk=hT(f('L2_attn_Wk')), wv=hT(f('L2_attn_Wv')),
              wo=hT(f('L2_attn_Wo')), bqr=hrow(f('L2_attn_bq')), bkr=hrow(f('L2_attn_bk')),
              bvr=hrow(f('L2_attn_bv')), bor=hrow(f('L2_attn_bo')),
              cqt=cq2, sqt=sq2, ckt=ck2, skt=sk2, mtc=_mask_l2())
    res = RUN(nc, [{'x': x2[c]} for c in range(NCORE)], sh)
    s2 = np.stack([np.asarray(res[c]['so'], np.float32) for c in range(NCORE)])

    nc = _cached('l2f', build_l2_ffn)
    sh = dict(wf=hT(f('L2_ffn_wf')), uf=hT(f('L2_ffn_uf')), wi=hT(f('L2_ffn_wi')),
              ui=hT(f('L2_ffn_ui')), wc=hT(f('L2_ffn_wc')), uc=hT(f('L2_ffn_uc')),
              wo=hT(f('L2_ffn_wo')), uo=hT(f('L2_ffn_uo')),
              bfr=hrow(f('L2_ffn_bf')), bir=hrow(f('L2_ffn_bi')), bcr=hrow(f('L2_ffn_bc')),
              bor=hrow(f('L2_ffn_bo')), incr=hrow(f('L2_ffn_in_c')))
    pc = [{'x': np.ascontiguousarray(x2[:, c * 512:(c + 1) * 512, :]),
           's': np.ascontiguousarray(s2[:, c * 512:(c + 1) * 512, :])} for c in range(NCORE)]
    res = RUN(nc, pc, sh)
    x3 = np.concatenate([np.asarray(res[c]['out'], np.float32) for c in range(NCORE)], axis=1)

    nc = _cached('l3a', build_l3_attn)
    sh = dict(convr=np.ascontiguousarray(f('L3_attn_conv_w').T), wq=f('L3_attn_Wq'),
              wkg=f('L3_attn_Wkg'), wv=f('L3_attn_Wv'), wg=f('L3_attn_Wg'),
              bgr=hrow(f('L3_attn_bg')), wout=f('L3_attn_Wout'), augr=hrow(f('L3_attn_aug_balance')))
    sh = {k: np.ascontiguousarray(v) for k, v in sh.items()}
    pc = [{'x': np.ascontiguousarray(x3[:, c * 512:(c + 1) * 512, :])} for c in range(NCORE)]
    res = RUN(nc, pc, sh)
    s3 = np.concatenate([np.asarray(res[c]['so'], np.float32) for c in range(NCORE)], axis=1)

    w1 = np.ascontiguousarray(f('L3_ffn_W1'))
    w2 = np.ascontiguousarray(f('L3_ffn_W2'))
    nc = _cached('l3f0', lambda: build_l3f(0))
    sh = dict(wgr=hT(f('L3_ffn_Wg')), ew1=w1[0:8], ew2=w2[0:8])
    res = RUN(nc, [{'x': x3[c]} for c in range(NCORE)], sh)
    z3 = np.stack([np.asarray(res[c]['z'], np.float32) for c in range(NCORE)])
    nc = _cached('l3f1', lambda: build_l3f(1))
    sh = dict(wgr=hT(f('L3_ffn_Wg')), ew1=w1[8:16], ew2=w2[8:16],
              ws1=np.ascontiguousarray(f('L3_ffn_Ws1')), ws2=np.ascontiguousarray(f('L3_ffn_Ws2')))
    res = RUN(nc, [{'x': x3[c], 'z': z3[c], 's3': s3[c]} for c in range(NCORE)], sh)
    x4 = np.stack([np.asarray(res[c]['xo'], np.float32) for c in range(NCORE)])
    return x4
